# version 14
# baseline (speedup 1.0000x reference)
import numpy as np
import ml_dtypes
from contextlib import ExitStack
from dataclasses import dataclass
import concourse.bass as bass
import concourse.mybir as mybir
from concourse.bass_utils import run_bass_kernel_spmd

F32 = mybir.dt.float32
BF16 = mybir.dt.bfloat16
I32 = mybir.dt.int32
I16 = mybir.dt.int16
AX = mybir.AxisListType.X
AF = mybir.ActivationFunctionType
OP = mybir.AluOpType

MAGIC = float(np.float32(1.5 * 2.0 ** 23))
NEG_BIG = -1.0e30
EPS_LN = 1e-5
EPS_Q = 1e-5
QB = 127.0

S = 2048
NSEQ = 2
D = 1024
E3 = 3072
H = 4096
NT = S // 128
DC = D // 128
EC3 = E3 // 128
HC = H // 128


def _windows():
    w = [(0, 512)]
    t = 512
    while t < S:
        L = min(507, S - t)
        w.append((t, L))
        t += L
    return w


WINDOWS = _windows()
ENGINES = ("sync", "gpsimd", "vector", "scalar", "tensor")


@dataclass
class _Op:
    idx: int
    engine: str
    emit: object
    deps: list
    dma: bool
    fence: bool = False
    needs_inc: bool = False
    count_after: int = 0
    pool: int = -1
    pool_target: int = 0


DMA_POOLS = {"sync": 8, "gpsimd": 4}


class Prog:

    def __init__(self, nc, ctx):
        self.nc = nc
        self.ops = []
        self.gate = []
        self.sems = {en: ctx.enter_context(nc.semaphore(f"ms_{en}")) for en in ENGINES}
        self.pool_sems = {}
        for en, k in DMA_POOLS.items():
            self.pool_sems[en] = [ctx.enter_context(nc.semaphore(f"dp_{en}{j}"))
                                  for j in range(k)]

    def op(self, engine, emit, deps=(), dma=False, fence=False, nogate=False):
        o = _Op(len(self.ops), engine, emit,
                [d for d in deps if d is not None] + ([] if nogate else self.gate),
                dma, fence)
        self.ops.append(o)
        return o

    def set_gate(self, deps):
        self.gate = [d for d in deps if d is not None]

    def emit(self, block):
        for o in self.ops:
            for d in o.deps:
                if d.engine != o.engine or d.dma or d.fence or d.engine != "tensor":
                    d.needs_inc = True
        counts = {en: 0 for en in ENGINES}
        pool_rr = {en: 0 for en in DMA_POOLS}
        pool_uses = {en: [0] * k for en, k in DMA_POOLS.items()}
        for o in self.ops:
            if o.dma:
                assert o.engine in DMA_POOLS, f"dma on engine {o.engine}"
                p = pool_rr[o.engine]
                pool_rr[o.engine] = (p + 1) % DMA_POOLS[o.engine]
                o.pool = p
                pool_uses[o.engine][p] += 1
                o.pool_target = 16 * pool_uses[o.engine][p]
            elif o.needs_inc:
                counts[o.engine] += 1
            o.count_after = counts[o.engine]
        by_eng = {en: [] for en in ENGINES}
        for o in self.ops:
            by_eng[o.engine].append(o)

        def emit_engine(eng_name, eng):
            wm_eng = {en: 0 for en in ENGINES}
            wm_pool = {(en, j): 0 for en, k in DMA_POOLS.items() for j in range(k)}

            def wait_for(d):
                if d.dma:
                    key = (d.engine, d.pool)
                    if d.pool_target > wm_pool[key]:
                        eng.wait_ge(self.pool_sems[d.engine][d.pool], d.pool_target)
                        wm_pool[key] = d.pool_target
                else:
                    if d.count_after > wm_eng[d.engine]:
                        eng.wait_ge(self.sems[d.engine], d.count_after)
                        wm_eng[d.engine] = d.count_after

            for o in by_eng[eng_name]:
                for d in o.deps:
                    if d.engine == eng_name and eng_name == "tensor" and not d.dma:
                        continue
                    wait_for(d)
                if o.dma:
                    key = (eng_name, o.pool)
                    prev = o.pool_target - 16
                    if prev > wm_pool[key]:
                        eng.wait_ge(self.pool_sems[eng_name][o.pool], prev)
                        wm_pool[key] = prev
                instr = o.emit(eng)
                if o.dma:
                    instr.then_inc(self.pool_sems[eng_name][o.pool], 16)
                elif o.needs_inc:
                    instr.then_inc(self.sems[eng_name], 1)

        section = {"sync": block.sync, "gpsimd": block.gpsimd,
                   "vector": block.vector, "scalar": block.scalar,
                   "tensor": block.tensor}
        for en in ENGINES:
            if by_eng[en]:
                section[en](lambda eng, _en=en: emit_engine(_en, eng))

    def finish(self, deps, engine="sync"):
        return self.op(engine, lambda e: e.nop(), deps=deps)


class Ring:
    def __init__(self, n):
        self.n = n
        self.users = [[] for _ in range(n)]

    def slot(self, i):
        s = i % self.n
        prev = list(self.users[s])
        self.users[s] = []
        return s, prev

    def use(self, i, op):
        if op is not None:
            self.users[i % self.n].append(op)


def build_nc(n_sub=4):
    nc = bass.Bass("TRN2", target_bir_lowering=False)
    x_in = nc.dram_tensor("x_in", [NSEQ, S, D], F32, kind="ExternalInput").ap()
    W_in = nc.dram_tensor("W_in", [D, E3], F32, kind="ExternalInput").ap()
    b_in = nc.dram_tensor("b_in", [E3], F32, kind="ExternalInput").ap()
    short_w = nc.dram_tensor("short_w", [E3, 3], F32, kind="ExternalInput").ap()
    short_b = nc.dram_tensor("short_b", [E3], F32, kind="ExternalInput").ap()
    inner_w = nc.dram_tensor("inner_w", [D, 4], F32, kind="ExternalInput").ap()
    inner_b = nc.dram_tensor("inner_b", [D], F32, kind="ExternalInput").ap()
    W_out = nc.dram_tensor("W_out", [D, D], F32, kind="ExternalInput").ap()
    b_out = nc.dram_tensor("b_out", [D], F32, kind="ExternalInput").ap()
    W_qkv = nc.dram_tensor("W_qkv", [E3, D], F32, kind="ExternalInput").ap()
    b_qkv = nc.dram_tensor("b_qkv", [E3], F32, kind="ExternalInput").ap()
    W1 = nc.dram_tensor("W1", [H, D], F32, kind="ExternalInput").ap()
    b1 = nc.dram_tensor("b1", [H], F32, kind="ExternalInput").ap()
    W2 = nc.dram_tensor("W2", [D, H], F32, kind="ExternalInput").ap()
    b2 = nc.dram_tensor("b2", [D], F32, kind="ExternalInput").ap()
    ident = nc.dram_tensor("ident", [128, 128], BF16, kind="ExternalInput").ap()
    causal = nc.dram_tensor("causal", [128, 128], F32, kind="ExternalInput").ap()
    y_out = nc.dram_tensor("y_out", [NSEQ, S, D], F32, kind="ExternalOutput").ap()
    xscr_a = nc.dram_tensor("xscr_a", [S, D], F32, kind="Internal").ap()
    xscr_b = nc.dram_tensor("xscr_b", [S, D], F32, kind="Internal").ap()
    wqkvT_s = nc.dram_tensor("wqkvT_s", [DC * 128, E3], BF16, kind="Internal").ap()
    win_s = nc.dram_tensor("win_s", [DC * 128, E3], BF16, kind="Internal").ap()
    t1_s = nc.dram_tensor("t1_s", [DC * 128, H], BF16, kind="Internal").ap()
    t2_s = nc.dram_tensor("t2_s", [HC * 128, D], BF16, kind="Internal").ap()
    hiT_s = nc.dram_tensor("hiT_s", [NT, 128, H], BF16, kind="Internal").ap()

    ctx = ExitStack()
    Eb = ctx.enter_context

    def sb(name, shape, dt=F32):
        return Eb(nc.sbuf_tensor(name, shape, dt))

    _uid = [0]

    def rsb(name, shape, dt=F32):
        _uid[0] += 1
        return nc.sbuf_tensor(f"{name}_{_uid[0]}", shape, dt)

    identt = sb("identt", [128, 128], BF16)
    causalt = sb("causalt", [128, 128], F32)
    ones_col = sb("ones_col", [128, 1], BF16)
    ones_col_f = sb("ones_col_f", [128, 1], F32)
    ones_row = sb("ones_row", [1, 128], BF16)
    ones_row_f = sb("ones_row_f", [1, 128], F32)
    half_col = sb("half_col", [128, 1], F32)
    b_in_pp = sb("b_in_pp", [128, EC3])
    short_w_pp = sb("short_w_pp", [128, EC3 * 3])
    short_b_pp = sb("short_b_pp", [128, EC3])
    inner_w_pp = sb("inner_w_pp", [128, DC * 4])
    inner_b_pp = sb("inner_b_pp", [128, DC])
    bq_pp = sb("bq_pp", [128, 2 * DC])
    bout_row = sb("bout_row", [1, D], BF16)
    bv_row = sb("bv_row", [1, D], BF16)
    b1_row = sb("b1_row", [1, H], F32)
    b2_row = sb("b2_row", [1, D], F32)
    b1w_row = sb("b1w_row", [1, H], BF16)
    b2w_row = sb("b2w_row", [1, D], BF16)
    ws1_pp = sb("ws1_pp", [128, 1])
    ws1i_pp = sb("ws1i_pp", [128, 1])
    ws2_pp = sb("ws2_pp", [128, 1])
    ws2i_pp = sb("ws2i_pp", [128, 1])
    st6 = sb("st6", [128, 24])
    mvbuf = sb("mvbuf", [128, 2 * NT])
    rstdb = sb("rstdb", [128, NT])
    negmb = sb("negmb", [128, NT])
    st2 = sb("st2", [128, 4])
    colA = sb("colA", [128, 2])
    colB = sb("colB", [128, 2])
    colC = sb("colC", [128, 2])
    colD = sb("colD", [128, 2])
    psA = Eb(nc.psum_tensor("psA", [128, 512], F32))
    psB = Eb(nc.psum_tensor("psB", [128, 512], F32))
    psC = Eb(nc.psum_tensor("psC", [128, 512], F32))
    psD = Eb(nc.psum_tensor("psD", [128, 512], F32))
    psT = Eb(nc.psum_tensor("psT", [128, 512], F32))
    psU = Eb(nc.psum_tensor("psU", [128, 512], F32))
    psK = Eb(nc.psum_tensor("psK", [128, 512], F32))

    P = Prog(nc, ctx)
    block = Eb(nc.Block())
    LP = {"psA": None, "psB": None, "psC": None, "psD": None, "psT": None,
          "psU": None, "psK": None}

    def dma(engine, out_ap, in_ap, deps=(), slow=False):
        return P.op(engine, lambda e: e.dma_start(out=out_ap, in_=in_ap,
                                                  allow_slow_non_contiguous=slow),
                    deps=deps, dma=True)

    xring = sb("xring", [128, 2 * D])
    xring2 = sb("xring2", [128, 2 * D])
    oring = sb("oring", [128, 2 * D])
    xlnt = sb("xlnt", [128, 2 * D], BF16)
    ring_in = Ring(2)
    ring_x2 = Ring(2)
    ring_o = Ring(2)

    def ln_tile(xt_ap, out_ap, deps, par=0, nogate=False):
        o6, o2 = par * 12, par * 2
        pA, pB, pC = colA[:, par:par + 1], colB[:, par:par + 1], colC[:, par:par + 1]
        s0 = P.op("vector", lambda e: e.bn_stats(st6[:, o6:o6 + 6], xt_ap[:, 0:512]), deps=deps, fence=True, nogate=nogate)
        s1 = P.op("vector", lambda e: e.bn_stats(st6[:, o6 + 6:o6 + 12], xt_ap[:, 512:1024]), deps=deps + [s0], fence=True, nogate=nogate)
        s2 = P.op("vector", lambda e: e.bn_aggr(st2[:, o2:o2 + 2], st6[:, o6:o6 + 12]), deps=[s0, s1], fence=True, nogate=nogate)
        s3 = P.op("vector", lambda e: e.tensor_single_scalar(pA, st2[:, o2 + 1:o2 + 2], EPS_LN, OP.add), deps=[s2], fence=True, nogate=nogate)
        s4 = P.op("scalar", lambda e: e.activation(pA, pA, AF.Ln), deps=[s3], fence=True, nogate=nogate)
        s5 = P.op("scalar", lambda e: e.activation(pB, pA, AF.Exp, scale=-0.5), deps=[s4], fence=True, nogate=nogate)
        s6 = P.op("vector", lambda e: e.scalar_tensor_tensor(pC, st2[:, o2:o2 + 1], -1.0, pB, OP.mult, OP.mult), deps=[s5, s2], fence=True, nogate=nogate)
        return P.op("scalar", lambda e: e.activation(out_ap, xt_ap, AF.Identity, bias=pC, scale=pB), deps=[s6] + deps, nogate=nogate)

    def load_x_tile(ring, buf, i, src_ap, extra=(), nogate=False):
        sl, prev = ring.slot(i)
        ld = P.op("sync", (lambda e, b=buf[:, sl * D:(sl + 1) * D],
                           a=src_ap[i * 128:(i + 1) * 128, :]: e.dma_start(out=b, in_=a)),
                  deps=list(extra) + prev, dma=True, nogate=nogate)
        return sl, ld

    _lid = [None]

    def transpose_blocks(src_fn, nblk, dst_fn, deps):
        last = None
        j = 0
        k = 0
        while j < nblk:
            hi = min(j + 4, nblk)
            ps, psn = (psT, "psT") if k % 2 == 0 else (psU, "psU")
            k += 1
            tdep = list(deps) + [_lid[0]] + ([LP[psn]] if LP[psn] else [])
            tlast = None
            for b in range(j, hi):
                tlast = P.op("tensor",
                             (lambda e, b=b, j=j, ps=ps: e.matmul(
                                 ps[:, (b - j) * 128:(b - j + 1) * 128], src_fn(b),
                                 identt[:], start=True, stop=True)),
                             deps=(tdep if b == j else []))

            def ev_emit(e, j=j, hi=hi, ps=ps):
                r = None
                for b in range(j, hi):
                    r = dst_fn(e, b, (b - j) * 128, ps)
                return r
            ev = P.op("scalar", ev_emit, deps=[tlast])
            LP[psn] = ev
            last = ev
            j = hi
        return last

    def transpose_pair_blocks(srcA, srcB, nblk, dst_fn, deps, evac_deps=()):
        last = None
        j = 0
        k = 0
        while j < nblk:
            hi = min(j + 4, nblk)
            ps, psn = (psT, "psT") if k % 2 == 0 else (psU, "psU")
            k += 1
            tdep = list(deps) + [_lid[0]] + ([LP[psn]] if LP[psn] else [])
            tlast = None
            for b in range(j, hi):
                P.op("tensor", (lambda e, b=b, j=j, ps=ps: e.matmul(
                    ps[:, (b - j) * 128:(b - j + 1) * 128],
                    srcA[:, b * 128:(b + 1) * 128], identt[:],
                    start=True, stop=False)), deps=(tdep if b == j else []))
                tlast = P.op("tensor", (lambda e, b=b, j=j, ps=ps: e.matmul(
                    ps[:, (b - j) * 128:(b - j + 1) * 128],
                    srcB[:, b * 128:(b + 1) * 128], identt[:],
                    start=False, stop=True)), deps=[])

            def ev_emit(e, j=j, hi=hi, ps=ps):
                r = None
                for b in range(j, hi):
                    r = dst_fn(e, b, (b - j) * 128, ps)
                return r
            ev = P.op("scalar", ev_emit, deps=[tlast] + list(evac_deps))
            LP[psn] = ev
            last = ev
            j = hi
        return last

    l_id = dma("sync", identt[:], ident)
    _lid[0] = l_id
    l_ca = dma("sync", causalt[:], causal)
    o_ones_b = P.op("vector", lambda e: (e.memset(ones_col[:], 1.0), e.memset(ones_row[:], 1.0))[1])
    o_ones_f = P.op("vector", lambda e: (e.memset(ones_col_f[:], 1.0), e.memset(ones_row_f[:], 1.0))[1])
    o_half = P.op("vector", lambda e: e.memset(half_col[:], 0.5))
    l_bin = dma("sync", b_in_pp[:], b_in.rearrange("(c p) -> p c", p=128), slow=True)
    l_sw = dma("sync", short_w_pp[:], short_w.rearrange("(c p) k -> p c k", p=128), slow=True)
    l_sb = dma("sync", short_b_pp[:], short_b.rearrange("(c p) -> p c", p=128), slow=True)
    l_iw = dma("sync", inner_w_pp[:], inner_w.rearrange("(c p) k -> p c k", p=128), slow=True)
    l_ib = dma("sync", inner_b_pp[:], inner_b.rearrange("(c p) -> p c", p=128), slow=True)
    l_bqk = dma("sync", bq_pp[:], b_qkv[0:2 * D].rearrange("(c p) -> p c", p=128), slow=True)
    l_bo = dma("gpsimd", bout_row[:], b_out[None, :])
    l_bv = dma("gpsimd", bv_row[:], b_qkv[None, 2 * D:3 * D])
    l_b1r = dma("sync", b1_row[:], b1[None, :])
    l_b2r = dma("sync", b2_row[:], b2[None, :])

    def bias_bcast(row_ap, dst, n, deps):
        last = None
        for c0 in range(0, n, 512):
            w = min(512, n - c0)
            dep0 = list(deps) + [o_ones_b] + ([LP["psK"]] if LP["psK"] else [])
            m = P.op("tensor", (lambda e, c0=c0, w=w: e.matmul(
                psK[:, 0:w], ones_row[0:1, :], row_ap[0:1, c0:c0 + w], start=True, stop=True)), deps=dep0)
            ev = P.op("vector", (lambda e, c0=c0, w=w: e.tensor_copy(dst[:, c0:c0 + w], psK[:, 0:w])), deps=[m])
            LP["psK"] = ev
            last = ev
        return last


    with nc.sbuf_tensor("wload", [128, 2 * H], F32) as wload, \
         nc.sbuf_tensor("tA", [128, H], BF16) as tA, \
         nc.sbuf_tensor("tB", [128, H], BF16) as tB, \
         nc.sbuf_tensor("tern", [128, 2 * H], BF16) as tern, \
         nc.sbuf_tensor("accv", [128, 1], F32) as accv, \
         nc.sbuf_tensor("stage", [128, DC * H], BF16) as stage, \
         nc.sbuf_tensor("wscan", [128, 2 * 1024], F32) as wscan, \
         nc.sbuf_tensor("wqload", [128, 2 * D], BF16) as wqload:

        wc_prev = [None, None]
        wc_last = None
        for part in range(3):
            for i in range(DC):
                par = i % 2
                wq_s = wqload[:, par * D:(par + 1) * D]
                dep = [wc_prev[par]] if wc_prev[par] else []
                ldc = dma("gpsimd", wq_s, W_in[i * 128:(i + 1) * 128, part * D:(part + 1) * D], deps=dep)
                stc = dma("sync", win_s[i * 128:(i + 1) * 128, part * D:(part + 1) * D], wq_s, deps=[ldc])
                wc_prev[par] = stc
                wc_last = stc
        lastq = None
        wqload2 = stage[:, DC * E3:DC * E3 + 2 * D]
        q_guard = [None, None]
        for i in range(EC3):
            par = i % 2
            wq_s = wqload2[:, par * D:(par + 1) * D]
            dep = [q_guard[par]] if q_guard[par] else []
            ld = dma("gpsimd", wq_s, W_qkv[i * 128:(i + 1) * 128, :], deps=dep)
            lastq = transpose_blocks(
                lambda bb, wq_s=wq_s: wq_s[:, bb * 128:(bb + 1) * 128], DC,
                (lambda e, bb, pc, ps, i=i: e.copy(stage[:, bb * E3 + i * 128: bb * E3 + i * 128 + 128], ps[:, pc:pc + 128])),
                [ld])
            q_guard[par] = lastq
        stq = dma("sync", wqkvT_s.rearrange("(dc p) e -> p dc e", p=128), stage[:, 0:DC * E3], deps=[lastq])

        def wscale_pass(W_ap, ntiles, tilecols, ws_pp, wsi_pp, guard):
            z = P.op("vector", lambda e: e.memset(accv[:], 0.0), deps=[guard] if guard else [])
            prev = z
            slot_guard = [guard, guard]
            W_v = W_ap.rearrange("(a p) d -> p a d", p=128)
            chunks = []
            if tilecols >= 1024:
                for a in range(ntiles):
                    for d0 in range(0, tilecols, 1024):
                        chunks.append(W_v[:, a:a + 1, d0:d0 + 1024])
            else:
                ga = 1024 // tilecols
                for a0 in range(0, ntiles, ga):
                    chunks.append(W_v[:, a0:a0 + ga, :])
            for k, src in enumerate(chunks):
                par = k % 2
                w_s = wscan[:, par * 1024:par * 1024 + 1024]
                dep = [slot_guard[par]] if slot_guard[par] else []
                ld = dma("sync", w_s, src, deps=dep)
                rs = P.op("vector", lambda e, w_s=w_s: e.reduce_sum(colA[:, 0:1], w_s, AX, apply_absolute_value=True), deps=[ld, prev], fence=True)
                slot_guard[par] = rs
                prev = P.op("vector", lambda e: e.tensor_add(accv[:], accv[:], colA[:, 0:1]), deps=[rs], fence=True)
            dep0 = [prev, o_ones_f] + ([LP["psK"]] if LP["psK"] else [])
            m = P.op("tensor", lambda e: e.matmul(psK[0:1, 0:1], ones_col_f[:, 0:1], accv[:, 0:1], start=True, stop=True), deps=dep0)
            sc = P.op("vector", (lambda e, n=ntiles * 128 * tilecols: e.tensor_scalar(colA[0:1, 0:1], psK[0:1, 0:1], 1.0 / n, EPS_Q, OP.mult, OP.max)), deps=[m], fence=True)
            LP["psK"] = sc
            m2 = P.op("tensor", lambda e: e.matmul(psK[:, 0:1], ones_row_f[0:1, :], colA[0:1, 0:1], start=True, stop=True), deps=[sc, LP["psK"]])
            c1 = P.op("vector", lambda e: e.tensor_copy(ws_pp[:], psK[:, 0:1]), deps=[m2], fence=True)
            LP["psK"] = c1
            return P.op("vector", lambda e: e.reciprocal(wsi_pp[:], ws_pp[:]), deps=[c1], fence=True)

        ws1_done = wscale_pass(W1, H // 128, D, ws1_pp, ws1i_pp, None)
        ws2_done = wscale_pass(W2, D // 128, H, ws2_pp, ws2i_pp, ws1_done)
        bw1 = P.op("vector", lambda e: e.tensor_scalar_mul(b1w_row[:], b1_row[:], ws1i_pp[0:1, 0:1]), deps=[ws1_done, l_b1r])
        bw2 = P.op("vector", lambda e: e.tensor_scalar_mul(b2w_row[:], b2_row[:], ws2i_pp[0:1, 0:1]), deps=[ws2_done, l_b2r])

        def tern_build(W_ap, wsi, wsi_done, ntiles, tilecols, nblk, col_of, buf_guard, stage_guard):
            last_ev = None
            ld_guard = [buf_guard, buf_guard]
            tern_guard = [buf_guard, buf_guard]
            sg = stage_guard
            t_prev = None
            for i in range(ntiles):
                par = i % 2
                w_s = wload[:, par * H:par * H + tilecols]
                t_s = tern[:, par * H:par * H + tilecols]
                gd = [ld_guard[par]] if ld_guard[par] else []
                ld = dma("gpsimd", w_s, W_ap[i * 128:(i + 1) * 128, :], deps=gd)
                a = P.op("vector", lambda e, tc=tilecols, w_s=w_s: e.tensor_scalar(tA[:, 0:tc], w_s, wsi[:, 0:1], 0.5, OP.mult, OP.is_gt), deps=[ld, t_prev, wsi_done])
                b = P.op("vector", lambda e, tc=tilecols, w_s=w_s: e.tensor_scalar(tB[:, 0:tc], w_s, wsi[:, 0:1], -0.5, OP.mult, OP.is_lt), deps=[ld, t_prev, wsi_done])
                ld_guard[par] = b
                t = P.op("vector", lambda e, tc=tilecols, t_s=t_s: e.tensor_sub(t_s, tA[:, 0:tc], tB[:, 0:tc]), deps=[a, b, tern_guard[par]])
                t_prev = t
                last_ev = transpose_blocks(
                    lambda bb, t_s=t_s: t_s[:, bb * 128:(bb + 1) * 128], nblk,
                    (lambda e, bb, pc, ps, i=i: e.copy(stage[:, col_of(i, bb):col_of(i, bb) + 128], ps[:, pc:pc + 128])),
                    [t] + sg)
                sg = []
                tern_guard[par] = last_ev
            return last_ev

        e1 = tern_build(W1, ws1i_pp, ws1_done, H // 128, D, DC, lambda i, b: b * H + i * 128, None, [stq])
        st1 = dma("sync", t1_s.rearrange("(dc p) h -> p dc h", p=128), stage[:, 0:DC * H], deps=[e1])
        e2 = tern_build(W2, ws2i_pp, ws2_done, D // 128, H, HC, lambda i, b: b * D + i * 128, e1, [st1])
        st2_ = dma("sync", t2_s.rearrange("(hc p) d -> p hc d", p=128), stage[:, 0:HC * D], deps=[e2])

    prep_done = [st1, st2_, stq, bw1, bw2, wc_last, l_bin, l_sw, l_sb, l_iw, l_ib,
                 l_bqk, l_bo, l_bv, l_id, l_ca, ws1_done, ws2_done]

    def ln_transpose_all(src_ap, start_deps, xlnT, tile_deps=None):
        slot_guard = [None, None]
        trs = []
        for i in range(NT):
            par = i % 2
            xl_s = xlnt[:, par * D:(par + 1) * D]
            ng = bool(tile_deps)
            td = [tile_deps[i]] if tile_deps else list(start_deps)
            sl, ld = load_x_tile(ring_in, xring, i, src_ap, td, nogate=ng)
            ap = ln_tile(xring[:, sl * D:(sl + 1) * D], xl_s,
                         [ld] + ([slot_guard[par]] if slot_guard[par] else []), par, nogate=ng)
            ring_in.use(sl, ap)
            last = transpose_blocks(
                lambda b, xl_s=xl_s: xl_s[:, b * 128:(b + 1) * 128], DC,
                (lambda e, b, pc, ps, i=i: e.copy(xlnT[:, b * S + i * 128: b * S + i * 128 + 128], ps[:, pc:pc + 128])),
                [ap])
            slot_guard[par] = last
            trs.append(last)
        return trs

    def sublayer_attn_a(src_ap, dst_ap, start_deps, tile_deps=None):
        stores = []
        with rsb("xlnTa", [128, DC * S], BF16) as xlnT, \
             rsb("ybuf", [128, DC * S], BF16) as ybuf:
          with rsb("xpuc", [128, EC3 * 512], BF16) as xpuc, \
             rsb("kvb", [128, DC * 512], BF16) as kvb, \
             rsb("winb", [128, 2 * 4 * D], BF16) as winb, \
             rsb("diag3", [128, EC3 * 3 * 128], BF16) as diag3, \
             rsb("diag4", [128, DC * 4 * 128], BF16) as diag4:
            trs = ln_transpose_all(src_ap, start_deps, xlnT, tile_deps)
            tr_last = trs[-1]
            dg_last = None
            for c in range(EC3):
                for k in range(3):
                    dg_last = P.op("vector", (lambda e, c=c, k=k: e.tensor_scalar_mul(
                        diag3[:, (c * 3 + k) * 128:(c * 3 + k + 1) * 128], identt[:],
                        short_w_pp[:, c * 3 + k:c * 3 + k + 1])), deps=[l_id, l_sw])
            for c in range(DC):
                for k in range(4):
                    dg_last = P.op("vector", (lambda e, c=c, k=k: e.tensor_scalar_mul(
                        diag4[:, (c * 4 + k) * 128:(c * 4 + k + 1) * 128], identt[:],
                        inner_w_pp[:, c * 4 + k:c * 4 + k + 1])), deps=[l_id, l_iw])

            y_last = None
            win_prev = [None, None]
            prev_window_v = None
            for (o0, L) in WINDOWS:
                halo = 0 if o0 == 0 else 5
                xp0 = o0 - halo
                XL = L + halo
                uh = 0 if o0 == 0 else 3
                UL = L + uh
                ub = halo - uh
                ev_xp = []
                WG = 4
                state_lw = None
                for c in range(EC3):
                    ps, psn = (psA, "psA") if c % 2 == 0 else (psB, "psB")
                    wb = (c // WG) % 2
                    ci = c % WG
                    if ci == 0:
                        state_lw = dma("sync", winb[:, wb * WG * D:(wb + 1) * WG * D],
                                       win_s.rearrange("(dc p) e -> p dc e", p=128)[:, :, c * 128:(c + WG) * 128],
                                       deps=[wc_last] + ([win_prev[wb]] if win_prev[wb] else []))
                    lw = state_lw
                    tr_need = trs[min((o0 + L - 1) // 128, NT - 1)]
                    dep0 = [lw, tr_need, prev_window_v] + ([LP[psn]] if LP[psn] else [])
                    mlast = None
                    for dcc in range(DC):
                        mlast = P.op("tensor", (lambda e, dcc=dcc, ps=ps, wb=wb, ci=ci, xp0=xp0, XL=XL: e.matmul(
                            ps[:, 0:XL], winb[:, wb * WG * D + dcc * WG * 128 + ci * 128: wb * WG * D + dcc * WG * 128 + (ci + 1) * 128],
                            xlnT[:, dcc * S + xp0: dcc * S + xp0 + XL],
                            start=(dcc == 0), stop=(dcc == DC - 1))), deps=(dep0 if dcc == 0 else []))
                    if ci == WG - 1:
                        win_prev[wb] = mlast
                    ev = P.op("scalar", (lambda e, c=c, ps=ps, XL=XL: e.activation(
                        xpuc[:, c * 512:c * 512 + XL], ps[:, 0:XL], AF.Identity,
                        bias=b_in_pp[:, c:c + 1])), deps=[mlast, l_bin, prev_window_v])
                    LP[psn] = ev
                    ev_xp.append(ev)
                ev_uc = []
                for c in range(EC3):
                    ps, psn = (psC, "psC") if c % 2 == 0 else (psD, "psD")
                    dep0 = [ev_xp[c], dg_last] + ([LP[psn]] if LP[psn] else [])
                    mlast = None
                    first = True
                    for dlt in (0, 1, 2):
                        kk = 2 - dlt
                        if o0 == 0:
                            lo, src0, n = dlt, 0, UL - dlt
                        else:
                            lo, src0, n = 0, ub - dlt + 0, UL
                        if n <= 0:
                            continue
                        mlast = P.op("tensor", (lambda e, c=c, kk=kk, ps=ps, lo=lo, src0=src0, n=n, first=first, UL=UL: e.matmul(
                            ps[:, lo:lo + n],
                            diag3[:, (c * 3 + kk) * 128:(c * 3 + kk + 1) * 128],
                            xpuc[:, c * 512 + src0: c * 512 + src0 + n],
                            start=first, stop=(kk == 0))), deps=(dep0 if first else []))
                        first = False
                    ev = P.op("scalar", (lambda e, c=c, ps=ps, UL=UL, ub=ub: e.activation(
                        xpuc[:, c * 512 + ub: c * 512 + ub + UL], ps[:, 0:UL], AF.Identity,
                        bias=short_b_pp[:, c:c + 1])), deps=[mlast, l_sb])
                    LP[psn] = ev
                    ev_uc.append(ev)
                ev_kv = []
                for c in range(DC):
                    ev = P.op("vector", (lambda e, c=c, ub=ub, UL=UL: e.tensor_mul(
                        kvb[:, c * 512: c * 512 + UL],
                        xpuc[:, (8 + c) * 512 + ub: (8 + c) * 512 + ub + UL],
                        xpuc[:, (16 + c) * 512 + ub: (16 + c) * 512 + ub + UL])),
                        deps=[ev_uc[8 + c], ev_uc[16 + c]])
                    ev_kv.append(ev)
                ev_vi = []
                for c in range(DC):
                    ps, psn = (psC, "psC") if c % 2 == 0 else (psD, "psD")
                    dep0 = [ev_kv[c]] + ([LP[psn]] if LP[psn] else [])
                    mlast = None
                    first = True
                    for dlt in (0, 1, 2, 3):
                        kk = 3 - dlt
                        if o0 == 0:
                            lo, src0, n = dlt, 0, L - dlt
                        else:
                            lo, src0, n = 0, uh - dlt, L
                        if n <= 0:
                            continue
                        mlast = P.op("tensor", (lambda e, c=c, kk=kk, ps=ps, lo=lo, src0=src0, n=n, first=first: e.matmul(
                            ps[:, lo:lo + n],
                            diag4[:, (c * 4 + kk) * 128:(c * 4 + kk + 1) * 128],
                            kvb[:, c * 512 + src0: c * 512 + src0 + n],
                            start=first, stop=(kk == 0))), deps=(dep0 if first else []))
                        first = False
                    ev = P.op("scalar", (lambda e, c=c, ps=ps, L=L, uh=uh: e.activation(
                        kvb[:, c * 512 + uh: c * 512 + uh + L], ps[:, 0:L], AF.Identity,
                        bias=inner_b_pp[:, c:c + 1])), deps=[mlast, l_ib])
                    LP[psn] = ev
                    ev_vi.append(ev)
                for c in range(DC):
                    y_last = P.op("vector", (lambda e, c=c, o0=o0, L=L, uh=uh, ub=ub: e.tensor_mul(
                        ybuf[:, c * S + o0: c * S + o0 + L],
                        xpuc[:, c * 512 + ub + uh: c * 512 + ub + uh + L],
                        kvb[:, c * 512 + uh: c * 512 + uh + L])),
                        deps=[ev_vi[c], ev_uc[c]])
                prev_window_v = y_last
          with rsb("woutb", [128, DC * D], BF16) as woutb:
            lwo = dma("gpsimd", woutb[:], W_out.rearrange("(dc p) e -> p dc e", p=128), deps=[y_last])
            for i in range(NT):
                sl2, ld2 = load_x_tile(ring_x2, xring2, i, src_ap)
                so, prevo = ring_o.slot(i)
                evs = []
                for eh in range(2):
                    ps, psn = (psA, "psA") if eh == 0 else (psB, "psB")
                    dep0 = [lwo, l_bo, o_ones_b, y_last] + ([LP[psn]] if LP[psn] else [])
                    mlast = None
                    for dcc in range(DC):
                        mlast = P.op("tensor", (lambda e, dcc=dcc, i=i, eh=eh, ps=ps: e.matmul(
                            ps[:, 0:512], ybuf[:, dcc * S + i * 128: dcc * S + (i + 1) * 128],
                            woutb[:, dcc * D + eh * 512: dcc * D + eh * 512 + 512],
                            start=(dcc == 0), stop=False)), deps=(dep0 if dcc == 0 else []))
                    mlast = P.op("tensor", (lambda e, eh=eh, ps=ps: e.matmul(
                        ps[:, 0:512], ones_row[0:1, :], bout_row[0:1, eh * 512:eh * 512 + 512],
                        start=False, stop=True)), deps=[])
                    ev = P.op("vector", (lambda e, eh=eh, ps=ps, sl2=sl2, so=so: e.tensor_add(
                        oring[:, so * D + eh * 512: so * D + eh * 512 + 512], ps[:, 0:512],
                        xring2[:, sl2 * D + eh * 512: sl2 * D + eh * 512 + 512])),
                        deps=[mlast, ld2] + prevo)
                    LP[psn] = ev
                    ring_x2.use(sl2, ev)
                    evs.append(ev)
                stv = dma("sync", dst_ap[i * 128:(i + 1) * 128, :], oring[:, so * D:(so + 1) * D], deps=evs)
                ring_o.use(so, stv)
                stores.append(stv)
        return stores

    def sublayer_ffn(src_ap, dst_ap, start_deps, tile_deps=None):
        stores = []
        stores_A = []
        with rsb("t1b", [128, DC * H], BF16) as t1b, \
             rsb("xlnf2", [128, 2 * D], F32) as xlnf2, \
             rsb("kiA", [128, D], I16) as kiA, \
             rsb("xib", [128, D], BF16) as xib, \
             rsb("xiT", [128, DC * 128], BF16) as xiT, \
             rsb("hB", [128, H], F32) as hB, \
             rsb("kiH", [128, H], I16) as kiH, \
             rsb("hib", [128, H], BF16) as hib, \
             rsb("hqb", [128, H], BF16) as hqb, \
             rsb("hstage", [128, H], BF16) as hstage:
            lt1s = []
            for k4 in range(4):
                lt1s.append(dma("sync", t1b.ap().rearrange("p (dc h) -> p dc h", dc=DC)[:, :, k4 * 1024:(k4 + 1) * 1024],
                                t1_s.rearrange("(dc p) h -> p dc h", p=128)[:, :, k4 * 1024:(k4 + 1) * 1024],
                                deps=[st1] + list(start_deps)))
            sth_guard = [None]
            state = {}
            pv = None
            for i in range(NT):
                par = i % 2
                o6 = par * 12
                td = [tile_deps[i]] if tile_deps else list(start_deps)
                sl, ld = load_x_tile(ring_in, xring, i, src_ap, td, nogate=bool(tile_deps))
                xt = xring[:, sl * D:(sl + 1) * D]
                a0 = P.op("vector", lambda e, xt=xt, o6=o6: e.bn_stats(st6[:, o6:o6 + 6], xt[:, 0:512]), deps=[ld, pv], fence=True, nogate=True)
                a1 = P.op("vector", lambda e, xt=xt, o6=o6: e.bn_stats(st6[:, o6 + 6:o6 + 12], xt[:, 512:1024]), deps=[ld, a0], fence=True, nogate=True)
                pv = P.op("vector", lambda e, i=i, o6=o6: e.bn_aggr(mvbuf[:, 2 * i:2 * i + 2], st6[:, o6:o6 + 12]), deps=[a0, a1], fence=True, nogate=True)
                ring_in.use(sl, pv)
            b0 = P.op("vector", lambda e: e.tensor_single_scalar(rstdb[:], mvbuf.ap().rearrange("p (t two) -> p t two", two=2)[:, :, 1], EPS_LN, OP.add), deps=[pv], fence=True, nogate=True)
            b1_ = P.op("scalar", lambda e: e.activation(rstdb[:], rstdb[:], AF.Ln), deps=[b0], fence=True, nogate=True)
            b2_ = P.op("scalar", lambda e: e.activation(rstdb[:], rstdb[:], AF.Exp, scale=-0.5), deps=[b1_], fence=True, nogate=True)
            b3_ = P.op("vector", lambda e: e.scalar_tensor_tensor(negmb[:], mvbuf.ap().rearrange("p (t two) -> p t two", two=2)[:, :, 0], -1.0, rstdb[:], OP.mult, OP.mult), deps=[b2_], fence=True, nogate=True)

            def FRONT(i):
                par = i % 2
                pA, pB, pC = colA[:, par:par + 1], colB[:, par:par + 1], colC[:, par:par + 1]
                xlnf_s = xlnf2[:, par * D:(par + 1) * D]
                sl, ld = load_x_tile(ring_in, xring, i, src_ap, start_deps)
                ap = P.op("scalar", lambda e, i=i, xt=xring[:, sl * D:(sl + 1) * D], xs=xlnf_s: e.activation(
                    xs, xt, AF.Identity, bias=negmb[:, i:i + 1], scale=rstdb[:, i:i + 1]), deps=[ld, b3_, state.get("r5")])
                ring_in.use(sl, ap)
                q1 = P.op("vector", lambda e, xs=xlnf_s, pA=pA: e.reduce_max(pA, xs, AX, apply_absolute_value=True), deps=[ap], fence=True)
                q2 = P.op("vector", lambda e, pA=pA: e.tensor_single_scalar(pA, pA, EPS_Q, OP.max), deps=[q1], fence=True)
                q3 = P.op("vector", lambda e, pA=pA, pB=pB: e.reciprocal(pB, pA), deps=[q2], fence=True)
                q4 = P.op("vector", lambda e, pB=pB: e.tensor_single_scalar(pB, pB, QB, OP.mult), deps=[q3], fence=True)
                q5 = P.op("vector", lambda e, pA=pA, pC=pC: e.tensor_single_scalar(pC, pA, 1.0 / QB, OP.mult), deps=[q2], fence=True)
                uop = P.op("scalar", lambda e, xs=xlnf_s, pB=pB: e.activation(xs, xs, AF.Identity, bias=half_col[:, 0:1], scale=pB), deps=[q4, ap, o_half, state.get("r5")], fence=True)
                r2 = P.op("vector", lambda e, xs=xlnf_s: e.tensor_scalar(kiA[:], xs, MAGIC, MAGIC, OP.add, OP.subtract), deps=[uop])
                r3 = P.op("vector", lambda e, xs=xlnf_s: e.scalar_tensor_tensor(xib[:], kiA[:], 0.0, xs, OP.add, OP.is_gt), deps=[r2])
                r4 = P.op("vector", lambda e, xs=xlnf_s: e.scalar_tensor_tensor(xs, kiA[:], 0.0, xib[:], OP.add, OP.subtract), deps=[r3])
                r5 = P.op("vector", lambda e, pC=pC, xs=xlnf_s: e.tensor_scalar_mul(xib[:], xs, pC), deps=[r4, q5])
                state["r5"] = r5
                tr = transpose_blocks(lambda b: xib[:, b * 128:(b + 1) * 128], DC,
                                      (lambda e, b, pc, ps: e.copy(xiT[:, b * 128:(b + 1) * 128], ps[:, pc:pc + 128])),
                                      [r5])
                state[("tr", i)] = tr

            def MM1GEL(i):
                par = i % 2
                gel = []
                for hc2 in range(H // 512):
                    ps, psn = (psA, "psA") if hc2 % 2 == 0 else (psB, "psB")
                    dep0 = [state[("tr", i)], lt1s[hc2 // 2]] + ([LP[psn]] if LP[psn] else [])
                    for dcc in range(DC):
                        P.op("tensor", (lambda e, dcc=dcc, hc2=hc2, ps=ps: e.matmul(
                            ps[:, 0:512], xiT[:, dcc * 128:(dcc + 1) * 128],
                            t1b[:, dcc * H + hc2 * 512: dcc * H + hc2 * 512 + 512],
                            start=(dcc == 0), stop=False)), deps=(dep0 if dcc == 0 else []))
                    mlast = P.op("tensor", (lambda e, hc2=hc2, ps=ps: e.matmul(
                        ps[:, 0:512], ones_row[0:1, :], b1w_row[0:1, hc2 * 512: hc2 * 512 + 512],
                        start=False, stop=True)), deps=[])
                    ev = P.op("scalar", (lambda e, hc2=hc2, ps=ps: e.activation(
                        hB[:, hc2 * 512:hc2 * 512 + 512], ps[:, 0:512], AF.Gelu,
                        scale=ws1_pp[:, 0:1])), deps=[mlast, state.get("hBread")])
                    LP[psn] = ev
                    gel.append(ev)
                state[("gel", i)] = gel

            def BACKQV(i):
                par = i % 2
                pA, pB, pC = colA[:, par:par + 1], colB[:, par:par + 1], colC[:, par:par + 1]
                gel = state.pop(("gel", i))
                h1 = P.op("vector", lambda e, pA=pA: e.reduce_max(pA, hB[:], AX, apply_absolute_value=True), deps=gel, fence=True)
                h2 = P.op("vector", lambda e, pA=pA: e.tensor_single_scalar(pA, pA, EPS_Q, OP.max), deps=[h1], fence=True)
                h3 = P.op("vector", lambda e, pA=pA, pB=pB: e.reciprocal(pB, pA), deps=[h2], fence=True)
                h4 = P.op("vector", lambda e, pB=pB: e.tensor_single_scalar(pB, pB, QB, OP.mult), deps=[h3], fence=True)
                h5 = P.op("vector", lambda e, pA=pA, pC=pC: e.tensor_single_scalar(pC, pA, 1.0 / QB, OP.mult), deps=[h2], fence=True)
                uh = P.op("scalar", lambda e, pB=pB: e.activation(hB[:], hB[:], AF.Identity, bias=half_col[:, 0:1], scale=pB), deps=[h4, o_half, state.get("q2v")] + gel, fence=True)
                state["uh"] = uh
                k2 = P.op("vector", lambda e: e.tensor_scalar(kiH[:], hB[:], MAGIC, MAGIC, OP.add, OP.subtract), deps=[uh])
                g2 = P.op("vector", lambda e: e.scalar_tensor_tensor(hib[:], kiH[:], 0.0, hB[:], OP.add, OP.is_gt), deps=[k2])
                state["hBread"] = g2
                kq = P.op("vector", lambda e, pC=pC: e.tensor_scalar_mul(hqb[:], kiH[:], pC), deps=[g2, h5])
                gq = P.op("vector", lambda e, pC=pC: e.tensor_scalar(hib[:], hib[:], pC, -1.0, OP.mult, OP.mult), deps=[g2, h5])
                q2v = gq
                state["q2v"] = gq
                state[("kqgq", i)] = [kq, gq]

            def HTRANS(i):
                hz = transpose_pair_blocks(hqb, hib, HC,
                                           (lambda e, b, pc, ps: e.copy(hstage[:, b * 128:(b + 1) * 128], ps[:, pc:pc + 128])),
                                           state.pop(("kqgq", i)), evac_deps=[sth_guard[0]])
                sth = dma("sync", hiT_s[i, :, :], hstage[:], deps=[hz])
                sth_guard[0] = sth
                stores_A.append(sth)

            for i in range(NT):
                FRONT(i)
                if i >= 1:
                    BACKQV(i - 1)
                MM1GEL(i)
                if i >= 1:
                    HTRANS(i - 1)
            BACKQV(NT - 1)
            HTRANS(NT - 1)
        with rsb("t2b", [128, HC * D], BF16) as t2b, \
             rsb("hiTb", [128, 3 * H], BF16) as hiTb:
            lt2s = []
            for k4 in range(4):
                lt2s.append(dma("sync", t2b.ap().rearrange("p (hc d) -> p hc d", hc=HC)[:, k4 * 8:(k4 + 1) * 8, :],
                                t2_s.rearrange("(hc p) d -> p hc d", p=128)[:, k4 * 8:(k4 + 1) * 8, :],
                                deps=[st2_, stores_A[-1]]))
            ringB = Ring(3)
            for i in range(NT):
                slh, prevh = ringB.slot(i)
                ldh = dma("sync", hiTb[:, slh * H:(slh + 1) * H], hiT_s[i, :, :], deps=[stores_A[-1]] + prevh)
                sl2, ld2 = load_x_tile(ring_x2, xring2, i, src_ap)
                so, prevo = ring_o.slot(i)
                evs = []
                for eh in range(2):
                    ps, psn = (psC, "psC") if eh == 0 else (psD, "psD")
                    dep0 = [ldh, lt2s[0]] + ([LP[psn]] if LP[psn] else [])
                    for hcc in range(HC):
                        dd = dep0 if hcc == 0 else ([lt2s[hcc // 8]] if hcc % 8 == 0 else [])
                        P.op("tensor", (lambda e, hcc=hcc, eh=eh, ps=ps, slh=slh: e.matmul(
                            ps[:, 0:512], hiTb[:, slh * H + hcc * 128: slh * H + (hcc + 1) * 128],
                            t2b[:, hcc * D + eh * 512: hcc * D + eh * 512 + 512],
                            start=(hcc == 0), stop=False)), deps=dd)
                    mlast = P.op("tensor", (lambda e, eh=eh, ps=ps: e.matmul(
                        ps[:, 0:512], ones_row[0:1, :], b2w_row[0:1, eh * 512: eh * 512 + 512],
                        start=False, stop=True)), deps=[])
                    ringB.use(slh, mlast)
                    ev = P.op("vector", (lambda e, eh=eh, ps=ps, so=so, sl2=sl2: e.scalar_tensor_tensor(
                        oring[:, so * D + eh * 512: so * D + eh * 512 + 512], ps[:, 0:512],
                        ws2_pp[:, 0:1], xring2[:, sl2 * D + eh * 512: sl2 * D + eh * 512 + 512],
                        OP.mult, OP.add)), deps=[mlast, ld2] + prevo)
                    LP[psn] = ev
                    ring_x2.use(sl2, ev)
                    evs.append(ev)
                stv = dma("sync", dst_ap[i * 128:(i + 1) * 128, :], oring[:, so * D:(so + 1) * D], deps=evs)
                ring_o.use(so, stv)
                stores.append(stv)
        return stores

    def sublayer_attn_b(src_ap, dst_ap, start_deps, tile_deps=None):
        stores = []
        with rsb("qb", [128, DC * S], BF16) as qb, \
             rsb("kb", [128, DC * S], BF16) as kb, \
             rsb("vb", [128, NT * D], BF16) as vb:
          with rsb("xlnTb", [128, DC * S], BF16) as xlnT, \
             rsb("wqb", [128, 2 * D], BF16) as wqb, \
             rsb("wvb", [128, DC * 512], BF16) as wvb:
            trs = ln_transpose_all(src_ap, start_deps, xlnT, tile_deps)
            tr_last = trs[-1]
            qk_last = None
            wq_prev = [None, None]
            for part in range(2):
                dstb = qb if part == 0 else kb
                for c in range(DC):
                    wslot = c % 2
                    lw = dma("sync", wqb[:, wslot * D:(wslot + 1) * D],
                             wqkvT_s.rearrange("(dc p) e -> p dc e", p=128)[:, :, part * D + c * 128: part * D + (c + 1) * 128],
                             deps=[stq] + ([wq_prev[wslot]] if wq_prev[wslot] else []))
                    for tch in range(4):
                        ps, psn = (psA, "psA") if tch % 2 == 0 else (psB, "psB")
                        dep0 = [lw, trs[4 * tch + 3]] + ([LP[psn]] if LP[psn] else [])
                        mlast = None
                        for dcc in range(DC):
                            mlast = P.op("tensor", (lambda e, dcc=dcc, tch=tch, ps=ps, wslot=wslot: e.matmul(
                                ps[:, 0:512], wqb[:, wslot * D + dcc * 128: wslot * D + (dcc + 1) * 128],
                                xlnT[:, dcc * S + tch * 512: dcc * S + tch * 512 + 512],
                                start=(dcc == 0), stop=(dcc == DC - 1))), deps=(dep0 if dcc == 0 else []))
                        wq_prev[wslot] = mlast
                        ev = P.op("scalar", (lambda e, c=c, tch=tch, ps=ps, part=part, dstb=dstb: e.activation(
                            dstb[:, c * S + tch * 512: c * S + tch * 512 + 512], ps[:, 0:512],
                            AF.Identity, bias=bq_pp[:, part * DC + c: part * DC + c + 1])), deps=[mlast, l_bqk])
                        LP[psn] = ev
                        qk_last = ev
            v_last = None
            lwv_prev = wq_prev[0]
            for eh in range(2):
                lwv = dma("sync", wvb[:],
                          wqkvT_s.rearrange("(dc p) e -> p dc e", p=128)[:, :, 2 * D + eh * 512: 2 * D + eh * 512 + 512],
                          deps=[stq, lwv_prev, wq_prev[1]])
                for i in range(NT):
                    ps, psn = (psC, "psC") if i % 2 == 0 else (psD, "psD")
                    dep0 = [lwv, trs[i], o_ones_b, l_bv] + ([LP[psn]] if LP[psn] else [])
                    mlast = None
                    for dcc in range(DC):
                        mlast = P.op("tensor", (lambda e, dcc=dcc, i=i, eh=eh, ps=ps: e.matmul(
                            ps[:, 0:512], xlnT[:, dcc * S + i * 128: dcc * S + (i + 1) * 128],
                            wvb[:, dcc * 512: dcc * 512 + 512],
                            start=(dcc == 0), stop=False)), deps=(dep0 if dcc == 0 else []))
                    mlast = P.op("tensor", (lambda e, eh=eh, ps=ps: e.matmul(
                        ps[:, 0:512], ones_row[0:1, :], bv_row[0:1, eh * 512:eh * 512 + 512],
                        start=False, stop=True)), deps=[])
                    ev = P.op("vector", (lambda e, i=i, eh=eh, ps=ps: e.tensor_copy(
                        vb[:, i * D + eh * 512: i * D + eh * 512 + 512], ps[:, 0:512])), deps=[mlast])
                    LP[psn] = ev
                    v_last = ev
                lwv_prev = mlast
          with rsb("scoreb", [128, 2 * S], F32) as scoreb2, \
             rsb("probb", [128, S], BF16) as probb, \
             rsb("attT", [128, S], BF16) as attT:
            st8 = {}

            def SCORES(qi):
                nk = (qi + 1) * 128
                par = qi % 2
                scoreb = scoreb2[:, par * S:(par + 1) * S]
                sl2, ld2 = load_x_tile(ring_x2, xring2, qi, src_ap)
                so, prevo = ring_o.slot(qi)
                guard = st8.get(("m1", qi - 2))
                ev_sc = []
                for kb0 in range(0, nk, 512):
                    w = min(512, nk - kb0)
                    ps, psn = (psA, "psA") if (kb0 // 512) % 2 == 0 else (psB, "psB")
                    dep0 = [qk_last, v_last] + ([LP[psn]] if LP[psn] else [])
                    mlast = None
                    for dcc in range(DC):
                        mlast = P.op("tensor", (lambda e, dcc=dcc, qi=qi, kb0=kb0, w=w, ps=ps: e.matmul(
                            ps[:, 0:w], qb[:, dcc * S + qi * 128: dcc * S + (qi + 1) * 128],
                            kb[:, dcc * S + kb0: dcc * S + kb0 + w],
                            start=(dcc == 0), stop=(dcc == DC - 1))), deps=(dep0 if dcc == 0 else []))
                    dlo = qi * 128 - kb0
                    if 0 <= dlo < w:
                        evp = None
                        if dlo > 0:
                            evp = P.op("scalar", (lambda e, kb0=kb0, dlo=dlo, ps=ps, scoreb=scoreb: e.copy(
                                scoreb[:, kb0:kb0 + dlo], ps[:, 0:dlo])), deps=[mlast, guard])
                            ev_sc.append(evp)
                        evd = P.op("vector", (lambda e, kb0=kb0, dlo=dlo, ps=ps, scoreb=scoreb: e.scalar_tensor_tensor(
                            scoreb[:, kb0 + dlo:kb0 + dlo + 128], ps[:, dlo:dlo + 128], 0.0,
                            causalt[:], OP.add, OP.add)), deps=[mlast, l_ca, guard, evp])
                        ev_sc.append(evd)
                        ev = evd
                    else:
                        ev = P.op("scalar", (lambda e, kb0=kb0, w=w, ps=ps, scoreb=scoreb: e.copy(
                            scoreb[:, kb0:kb0 + w], ps[:, 0:w])), deps=[mlast, guard])
                        ev_sc.append(ev)
                    LP[psn] = ev
                st8[("evsc", qi)] = ev_sc
                st8[("xo", qi)] = (sl2, ld2, so, prevo)

            def SM(qi):
                nk = (qi + 1) * 128
                par = qi % 2
                scoreb = scoreb2[:, par * S:(par + 1) * S]
                pA, pB, pC, pD = (colA[:, par:par + 1], colB[:, par:par + 1],
                                  colC[:, par:par + 1], colD[:, par:par + 1])
                ev_sc = st8.pop(("evsc", qi))
                m1 = P.op("vector", (lambda e, nk=nk, pA=pA, scoreb=scoreb: e.reduce_max(pA, scoreb[:, 0:nk], AX)), deps=ev_sc, fence=True)
                st8[("m1", qi)] = m1
                m2 = P.op("vector", (lambda e, pA=pA, pB=pB: e.tensor_single_scalar(pB, pA, -1.0 / 32.0, OP.mult)), deps=[m1], fence=True)
                ex = P.op("scalar", (lambda e, nk=nk, pB=pB, pC=pC, scoreb=scoreb: e.activation(
                    probb[:, 0:nk], scoreb[:, 0:nk], AF.Exp, bias=pB,
                    scale=float(1.0 / 32.0), accum_out=pC)), deps=[m2, st8.get(("avT", qi - 1))], fence=True)
                rc = P.op("vector", (lambda e, pC=pC, pD=pD: e.reciprocal(pD, pC)), deps=[ex], fence=True)
                st8[("sm", qi)] = (ex, rc, pD)

            def TRANSAV(qi):
                ex, rc, pD = st8.pop(("sm", qi))
                sl2, ld2, so, prevo = st8.pop(("xo", qi))
                trp = transpose_blocks(lambda b: probb[:, b * 128:(b + 1) * 128], qi + 1,
                                       (lambda e, b, pc, ps: e.copy(attT[:, b * 128:(b + 1) * 128], ps[:, pc:pc + 128])),
                                       [ex])
                evs = []
                mlast = None
                for eh in range(2):
                    ps, psn = (psC, "psC") if eh == 0 else (psD, "psD")
                    dep0 = [trp, v_last] + ([LP[psn]] if LP[psn] else [])
                    for kt in range(qi + 1):
                        mlast = P.op("tensor", (lambda e, kt=kt, eh=eh, ps=ps, qi=qi: e.matmul(
                            ps[:, 0:512], attT[:, kt * 128:(kt + 1) * 128],
                            vb[:, kt * D + eh * 512: kt * D + eh * 512 + 512],
                            start=(kt == 0), stop=(kt == qi))), deps=(dep0 if kt == 0 else []))
                    ev = P.op("vector", (lambda e, eh=eh, ps=ps, so=so, sl2=sl2, pD=pD: e.scalar_tensor_tensor(
                        oring[:, so * D + eh * 512: so * D + eh * 512 + 512], ps[:, 0:512],
                        pD, xring2[:, sl2 * D + eh * 512: sl2 * D + eh * 512 + 512],
                        OP.mult, OP.add)), deps=[mlast, rc, ld2] + prevo)
                    LP[psn] = ev
                    ring_x2.use(sl2, ev)
                    evs.append(ev)
                st8[("avT", qi)] = mlast
                stv = dma("sync", dst_ap[qi * 128:(qi + 1) * 128, :], oring[:, so * D:(so + 1) * D], deps=evs)
                ring_o.use(so, stv)
                stores.append(stv)

            for qi in range(NT):
                SCORES(qi)
                if qi >= 1:
                    SM(qi - 1)
                    TRANSAV(qi - 1)
            SM(NT - 1)
            TRANSAV(NT - 1)
        return stores

    fns = [sublayer_attn_a, sublayer_ffn, sublayer_attn_b, sublayer_ffn][:n_sub]
    last_stores = prep_done
    first_phase = True
    for s in range(NSEQ):
        cur_src = x_in[s, :, :]
        scrs = [xscr_a, xscr_b]
        for k, fn in enumerate(fns):
            dst = y_out[s, :, :] if k == len(fns) - 1 else scrs[k % 2]
            gate = list(last_stores) if first_phase else last_stores[-1:]
            tds = None if first_phase or len(last_stores) != NT else list(last_stores)
            first_phase = False
            P.set_gate(gate)
            done = fn(cur_src, dst, gate, tile_deps=tds)
            last_stores = done
            cur_src = dst
    P.set_gate([])
    P.finish(last_stores[-2:])
    P.emit(block)
    ctx.close()
    return nc


def kernel(**inputs):
    xfull = np.ascontiguousarray(inputs["L0_attn_a_u"])
    weights = {
        "W_in": inputs["L0_attn_a_W_in"], "b_in": inputs["L0_attn_a_b_in"],
        "short_w": inputs["L0_attn_a_short_w"], "short_b": inputs["L0_attn_a_short_b"],
        "inner_w": inputs["L0_attn_a_inner_w"], "inner_b": inputs["L0_attn_a_inner_b"],
        "W_out": inputs["L0_attn_a_W_out"], "b_out": inputs["L0_attn_a_b_out"],
        "W_qkv": inputs["L0_attn_b_W_qkv"], "b_qkv": inputs["L0_attn_b_b_qkv"],
        "W1": inputs["L0_ffn_W1"], "b1": inputs["L0_ffn_b1"],
        "W2": inputs["L0_ffn_W2"], "b2": inputs["L0_ffn_b2"],
    }
    weights = {k: np.ascontiguousarray(v) for k, v in weights.items()}
    ident_np = np.eye(128, dtype=ml_dtypes.bfloat16)
    causal_np = np.where(np.tril(np.ones((128, 128), bool)), 0.0, NEG_BIG).astype(np.float32)
    nc = build_nc(4)
    in_maps = []
    for c in range(8):
        m = dict(weights)
        m["x_in"] = xfull[2 * c:2 * c + 2]
        m["ident"] = ident_np
        m["causal"] = causal_np
        in_maps.append(m)
    res = run_bass_kernel_spmd(nc, in_maps, core_ids=list(range(8)))
    out = np.concatenate([r["y_out"] for r in res.results], axis=0)
    return np.ascontiguousarray(out.astype(np.float32))


# version 15
# speedup vs baseline: 1.0937x; 1.0937x over previous
import numpy as np
import ml_dtypes
from contextlib import ExitStack
from dataclasses import dataclass
import concourse.bass as bass
import concourse.mybir as mybir
from concourse.bass_utils import run_bass_kernel_spmd

F32 = mybir.dt.float32
BF16 = mybir.dt.bfloat16
I32 = mybir.dt.int32
I16 = mybir.dt.int16
AX = mybir.AxisListType.X
AF = mybir.ActivationFunctionType
OP = mybir.AluOpType

MAGIC = float(np.float32(1.5 * 2.0 ** 23))
NEG_BIG = -1.0e30
EPS_LN = 1e-5
EPS_Q = 1e-5
QB = 127.0

S = 2048
NSEQ = 2
D = 1024
E3 = 3072
H = 4096
NT = S // 128
DC = D // 128
EC3 = E3 // 128
HC = H // 128


def _windows():
    w = [(0, 512)]
    t = 512
    while t < S:
        L = min(507, S - t)
        w.append((t, L))
        t += L
    return w


WINDOWS = _windows()
ENGINES = ("sync", "gpsimd", "vector", "scalar", "tensor")


@dataclass
class _Op:
    idx: int
    engine: str
    emit: object
    deps: list
    dma: bool
    fence: bool = False
    needs_inc: bool = False
    count_after: int = 0
    pool: int = -1
    pool_target: int = 0


DMA_POOLS = {"sync": 8, "gpsimd": 4}


class Prog:

    def __init__(self, nc, ctx):
        self.nc = nc
        self.ops = []
        self.gate = []
        self.sems = {en: ctx.enter_context(nc.semaphore(f"ms_{en}")) for en in ENGINES}
        self.pool_sems = {}
        for en, k in DMA_POOLS.items():
            self.pool_sems[en] = [ctx.enter_context(nc.semaphore(f"dp_{en}{j}"))
                                  for j in range(k)]

    def op(self, engine, emit, deps=(), dma=False, fence=False, nogate=False):
        o = _Op(len(self.ops), engine, emit,
                [d for d in deps if d is not None] + ([] if nogate else self.gate),
                dma, fence)
        self.ops.append(o)
        return o

    def set_gate(self, deps):
        self.gate = [d for d in deps if d is not None]

    def emit(self, block):
        for o in self.ops:
            for d in o.deps:
                if d.engine != o.engine or d.dma or d.fence or d.engine != "tensor":
                    d.needs_inc = True
        counts = {en: 0 for en in ENGINES}
        pool_rr = {en: 0 for en in DMA_POOLS}
        pool_uses = {en: [0] * k for en, k in DMA_POOLS.items()}
        for o in self.ops:
            if o.dma:
                assert o.engine in DMA_POOLS, f"dma on engine {o.engine}"
                p = pool_rr[o.engine]
                pool_rr[o.engine] = (p + 1) % DMA_POOLS[o.engine]
                o.pool = p
                pool_uses[o.engine][p] += 1
                o.pool_target = 16 * pool_uses[o.engine][p]
            elif o.needs_inc:
                counts[o.engine] += 1
            o.count_after = counts[o.engine]
        by_eng = {en: [] for en in ENGINES}
        for o in self.ops:
            by_eng[o.engine].append(o)

        def emit_engine(eng_name, eng):
            wm_eng = {en: 0 for en in ENGINES}
            wm_pool = {(en, j): 0 for en, k in DMA_POOLS.items() for j in range(k)}

            def wait_for(d):
                if d.dma:
                    key = (d.engine, d.pool)
                    if d.pool_target > wm_pool[key]:
                        eng.wait_ge(self.pool_sems[d.engine][d.pool], d.pool_target)
                        wm_pool[key] = d.pool_target
                else:
                    if d.count_after > wm_eng[d.engine]:
                        eng.wait_ge(self.sems[d.engine], d.count_after)
                        wm_eng[d.engine] = d.count_after

            for o in by_eng[eng_name]:
                for d in o.deps:
                    if d.engine == eng_name and eng_name == "tensor" and not d.dma:
                        continue
                    wait_for(d)
                if o.dma:
                    key = (eng_name, o.pool)
                    prev = o.pool_target - 16
                    if prev > wm_pool[key]:
                        eng.wait_ge(self.pool_sems[eng_name][o.pool], prev)
                        wm_pool[key] = prev
                instr = o.emit(eng)
                if o.dma:
                    instr.then_inc(self.pool_sems[eng_name][o.pool], 16)
                elif o.needs_inc:
                    instr.then_inc(self.sems[eng_name], 1)

        section = {"sync": block.sync, "gpsimd": block.gpsimd,
                   "vector": block.vector, "scalar": block.scalar,
                   "tensor": block.tensor}
        for en in ENGINES:
            if by_eng[en]:
                section[en](lambda eng, _en=en: emit_engine(_en, eng))

    def finish(self, deps, engine="sync"):
        return self.op(engine, lambda e: e.nop(), deps=deps)


class Ring:
    def __init__(self, n):
        self.n = n
        self.users = [[] for _ in range(n)]

    def slot(self, i):
        s = i % self.n
        prev = list(self.users[s])
        self.users[s] = []
        return s, prev

    def use(self, i, op):
        if op is not None:
            self.users[i % self.n].append(op)


def build_nc(n_sub=4):
    nc = bass.Bass("TRN2", target_bir_lowering=False)
    x_in = nc.dram_tensor("x_in", [NSEQ, S, D], F32, kind="ExternalInput").ap()
    W_in = nc.dram_tensor("W_in", [D, E3], F32, kind="ExternalInput").ap()
    b_in = nc.dram_tensor("b_in", [E3], F32, kind="ExternalInput").ap()
    short_w = nc.dram_tensor("short_w", [E3, 3], F32, kind="ExternalInput").ap()
    short_b = nc.dram_tensor("short_b", [E3], F32, kind="ExternalInput").ap()
    inner_w = nc.dram_tensor("inner_w", [D, 4], F32, kind="ExternalInput").ap()
    inner_b = nc.dram_tensor("inner_b", [D], F32, kind="ExternalInput").ap()
    W_out = nc.dram_tensor("W_out", [D, D], F32, kind="ExternalInput").ap()
    b_out = nc.dram_tensor("b_out", [D], F32, kind="ExternalInput").ap()
    W_qkv = nc.dram_tensor("W_qkv", [E3, D], F32, kind="ExternalInput").ap()
    b_qkv = nc.dram_tensor("b_qkv", [E3], F32, kind="ExternalInput").ap()
    W1 = nc.dram_tensor("W1", [H, D], F32, kind="ExternalInput").ap()
    b1 = nc.dram_tensor("b1", [H], F32, kind="ExternalInput").ap()
    W2 = nc.dram_tensor("W2", [D, H], F32, kind="ExternalInput").ap()
    b2 = nc.dram_tensor("b2", [D], F32, kind="ExternalInput").ap()
    ident = nc.dram_tensor("ident", [128, 128], BF16, kind="ExternalInput").ap()
    causal = nc.dram_tensor("causal", [128, 128], F32, kind="ExternalInput").ap()
    y_out = nc.dram_tensor("y_out", [NSEQ, S, D], F32, kind="ExternalOutput").ap()
    xscr_a = nc.dram_tensor("xscr_a", [S, D], F32, kind="Internal").ap()
    xscr_b = nc.dram_tensor("xscr_b", [S, D], F32, kind="Internal").ap()
    wqkvT_s = nc.dram_tensor("wqkvT_s", [DC * 128, E3], BF16, kind="Internal").ap()
    win_s = nc.dram_tensor("win_s", [DC * 128, E3], BF16, kind="Internal").ap()
    t1_s = nc.dram_tensor("t1_s", [DC * 128, H], BF16, kind="Internal").ap()
    t2_s = nc.dram_tensor("t2_s", [HC * 128, D], BF16, kind="Internal").ap()
    hiT_s = nc.dram_tensor("hiT_s", [NT, 128, H], BF16, kind="Internal").ap()

    ctx = ExitStack()
    Eb = ctx.enter_context

    def sb(name, shape, dt=F32):
        return Eb(nc.sbuf_tensor(name, shape, dt))

    _uid = [0]

    def rsb(name, shape, dt=F32):
        _uid[0] += 1
        return nc.sbuf_tensor(f"{name}_{_uid[0]}", shape, dt)

    identt = sb("identt", [128, 128], BF16)
    causalt = sb("causalt", [128, 128], F32)
    ones_col = sb("ones_col", [128, 1], BF16)
    ones_col_f = sb("ones_col_f", [128, 1], F32)
    ones_row = sb("ones_row", [1, 128], BF16)
    ones_row_f = sb("ones_row_f", [1, 128], F32)
    half_col = sb("half_col", [128, 1], F32)
    b_in_pp = sb("b_in_pp", [128, EC3])
    short_w_pp = sb("short_w_pp", [128, EC3 * 3])
    short_b_pp = sb("short_b_pp", [128, EC3])
    inner_w_pp = sb("inner_w_pp", [128, DC * 4])
    inner_b_pp = sb("inner_b_pp", [128, DC])
    bq_pp = sb("bq_pp", [128, 2 * DC])
    bout_row = sb("bout_row", [1, D], BF16)
    bv_row = sb("bv_row", [1, D], BF16)
    b1_row = sb("b1_row", [1, H], F32)
    b2_row = sb("b2_row", [1, D], F32)
    b1w_row = sb("b1w_row", [1, H], BF16)
    b2w_row = sb("b2w_row", [1, D], BF16)
    ws1_pp = sb("ws1_pp", [128, 1])
    ws1i_pp = sb("ws1i_pp", [128, 1])
    ws2_pp = sb("ws2_pp", [128, 1])
    ws2i_pp = sb("ws2i_pp", [128, 1])
    st6 = sb("st6", [128, 24])
    mvbuf = sb("mvbuf", [128, 2 * NT])
    rstdb = sb("rstdb", [128, NT])
    negmb = sb("negmb", [128, NT])
    st2 = sb("st2", [128, 4])
    colA = sb("colA", [128, 2])
    colB = sb("colB", [128, 2])
    colC = sb("colC", [128, 2])
    colD = sb("colD", [128, 2])
    psA = Eb(nc.psum_tensor("psA", [128, 512], F32))
    psB = Eb(nc.psum_tensor("psB", [128, 512], F32))
    psC = Eb(nc.psum_tensor("psC", [128, 512], F32))
    psD = Eb(nc.psum_tensor("psD", [128, 512], F32))
    psT = Eb(nc.psum_tensor("psT", [128, 512], F32))
    psU = Eb(nc.psum_tensor("psU", [128, 512], F32))
    psK = Eb(nc.psum_tensor("psK", [128, 512], F32))

    P = Prog(nc, ctx)
    block = Eb(nc.Block())
    LP = {"psA": None, "psB": None, "psC": None, "psD": None, "psT": None,
          "psU": None, "psK": None}

    def dma(engine, out_ap, in_ap, deps=(), slow=False):
        return P.op(engine, lambda e: e.dma_start(out=out_ap, in_=in_ap,
                                                  allow_slow_non_contiguous=slow),
                    deps=deps, dma=True)

    xring = sb("xring", [128, 2 * D])
    xring2 = sb("xring2", [128, 2 * D])
    oring = sb("oring", [128, 2 * D])
    xlnt = sb("xlnt", [128, 2 * D], BF16)
    ring_in = Ring(2)
    ring_x2 = Ring(2)
    ring_o = Ring(2)

    def ln_tile(xt_ap, out_ap, deps, par=0, nogate=False):
        o6, o2 = par * 12, par * 2
        pA, pB, pC = colA[:, par:par + 1], colB[:, par:par + 1], colC[:, par:par + 1]
        s0 = P.op("vector", lambda e: e.bn_stats(st6[:, o6:o6 + 6], xt_ap[:, 0:512]), deps=deps, fence=True, nogate=nogate)
        s1 = P.op("vector", lambda e: e.bn_stats(st6[:, o6 + 6:o6 + 12], xt_ap[:, 512:1024]), deps=deps + [s0], fence=True, nogate=nogate)
        s2 = P.op("vector", lambda e: e.bn_aggr(st2[:, o2:o2 + 2], st6[:, o6:o6 + 12]), deps=[s0, s1], fence=True, nogate=nogate)
        s3 = P.op("vector", lambda e: e.tensor_single_scalar(pA, st2[:, o2 + 1:o2 + 2], EPS_LN, OP.add), deps=[s2], fence=True, nogate=nogate)
        s4 = P.op("scalar", lambda e: e.activation(pA, pA, AF.Ln), deps=[s3], fence=True, nogate=nogate)
        s5 = P.op("scalar", lambda e: e.activation(pB, pA, AF.Exp, scale=-0.5), deps=[s4], fence=True, nogate=nogate)
        s6 = P.op("vector", lambda e: e.scalar_tensor_tensor(pC, st2[:, o2:o2 + 1], -1.0, pB, OP.mult, OP.mult), deps=[s5, s2], fence=True, nogate=nogate)
        return P.op("scalar", lambda e: e.activation(out_ap, xt_ap, AF.Identity, bias=pC, scale=pB), deps=[s6] + deps, nogate=nogate)

    def load_x_tile(ring, buf, i, src_ap, extra=(), nogate=False):
        sl, prev = ring.slot(i)
        ld = P.op("sync", (lambda e, b=buf[:, sl * D:(sl + 1) * D],
                           a=src_ap[i * 128:(i + 1) * 128, :]: e.dma_start(out=b, in_=a)),
                  deps=list(extra) + prev, dma=True, nogate=nogate)
        return sl, ld

    _lid = [None]

    def transpose_blocks(src_fn, nblk, dst_fn, deps, batch=False):
        last = None
        j = 0
        k = 0
        while j < nblk:
            hi = min(j + 4, nblk)
            ps, psn = (psT, "psT") if k % 2 == 0 else (psU, "psU")
            k += 1
            tdep = list(deps) + [_lid[0]] + ([LP[psn]] if LP[psn] else [])
            tlast = None
            for b in range(j, hi):
                tlast = P.op("tensor",
                             (lambda e, b=b, j=j, ps=ps: e.matmul(
                                 ps[:, (b - j) * 128:(b - j + 1) * 128], src_fn(b),
                                 identt[:], start=True, stop=True)),
                             deps=(tdep if b == j else []))

            def ev_emit(e, j=j, hi=hi, ps=ps):
                if batch:
                    return dst_fn(e, j, hi, ps)
                r = None
                for b in range(j, hi):
                    r = dst_fn(e, b, (b - j) * 128, ps)
                return r
            ev = P.op("scalar", ev_emit, deps=[tlast])
            LP[psn] = ev
            last = ev
            j = hi
        return last

    def transpose_pair_blocks(srcA, srcB, nblk, dst_fn, deps, evac_deps=(), batch=False):
        last = None
        j = 0
        k = 0
        while j < nblk:
            hi = min(j + 4, nblk)
            ps, psn = (psT, "psT") if k % 2 == 0 else (psU, "psU")
            k += 1
            tdep = list(deps) + [_lid[0]] + ([LP[psn]] if LP[psn] else [])
            tlast = None
            for b in range(j, hi):
                P.op("tensor", (lambda e, b=b, j=j, ps=ps: e.matmul(
                    ps[:, (b - j) * 128:(b - j + 1) * 128],
                    srcA[:, b * 128:(b + 1) * 128], identt[:],
                    start=True, stop=False)), deps=(tdep if b == j else []))
                tlast = P.op("tensor", (lambda e, b=b, j=j, ps=ps: e.matmul(
                    ps[:, (b - j) * 128:(b - j + 1) * 128],
                    srcB[:, b * 128:(b + 1) * 128], identt[:],
                    start=False, stop=True)), deps=[])

            def ev_emit(e, j=j, hi=hi, ps=ps):
                if batch:
                    return dst_fn(e, j, hi, ps)
                r = None
                for b in range(j, hi):
                    r = dst_fn(e, b, (b - j) * 128, ps)
                return r
            ev = P.op("scalar", ev_emit, deps=[tlast] + list(evac_deps))
            LP[psn] = ev
            last = ev
            j = hi
        return last

    l_id = dma("sync", identt[:], ident)
    _lid[0] = l_id
    l_ca = dma("sync", causalt[:], causal)
    o_ones_b = P.op("vector", lambda e: (e.memset(ones_col[:], 1.0), e.memset(ones_row[:], 1.0))[1])
    o_ones_f = P.op("vector", lambda e: (e.memset(ones_col_f[:], 1.0), e.memset(ones_row_f[:], 1.0))[1])
    o_half = P.op("vector", lambda e: e.memset(half_col[:], 0.5))
    l_bin = dma("sync", b_in_pp[:], b_in.rearrange("(c p) -> p c", p=128), slow=True)
    l_sw = dma("sync", short_w_pp[:], short_w.rearrange("(c p) k -> p c k", p=128), slow=True)
    l_sb = dma("sync", short_b_pp[:], short_b.rearrange("(c p) -> p c", p=128), slow=True)
    l_iw = dma("sync", inner_w_pp[:], inner_w.rearrange("(c p) k -> p c k", p=128), slow=True)
    l_ib = dma("sync", inner_b_pp[:], inner_b.rearrange("(c p) -> p c", p=128), slow=True)
    l_bqk = dma("sync", bq_pp[:], b_qkv[0:2 * D].rearrange("(c p) -> p c", p=128), slow=True)
    l_bo = dma("gpsimd", bout_row[:], b_out[None, :])
    l_bv = dma("gpsimd", bv_row[:], b_qkv[None, 2 * D:3 * D])
    l_b1r = dma("sync", b1_row[:], b1[None, :])
    l_b2r = dma("sync", b2_row[:], b2[None, :])

    def bias_bcast(row_ap, dst, n, deps):
        last = None
        for c0 in range(0, n, 512):
            w = min(512, n - c0)
            dep0 = list(deps) + [o_ones_b] + ([LP["psK"]] if LP["psK"] else [])
            m = P.op("tensor", (lambda e, c0=c0, w=w: e.matmul(
                psK[:, 0:w], ones_row[0:1, :], row_ap[0:1, c0:c0 + w], start=True, stop=True)), deps=dep0)
            ev = P.op("vector", (lambda e, c0=c0, w=w: e.tensor_copy(dst[:, c0:c0 + w], psK[:, 0:w])), deps=[m])
            LP["psK"] = ev
            last = ev
        return last


    with nc.sbuf_tensor("wload", [128, 2 * H], F32) as wload, \
         nc.sbuf_tensor("tA", [128, H], BF16) as tA, \
         nc.sbuf_tensor("tB", [128, H], BF16) as tB, \
         nc.sbuf_tensor("tern", [128, 2 * H], BF16) as tern, \
         nc.sbuf_tensor("accv", [128, 1], F32) as accv, \
         nc.sbuf_tensor("stage", [128, DC * H], BF16) as stage, \
         nc.sbuf_tensor("wscan", [128, 2 * 1024], F32) as wscan, \
         nc.sbuf_tensor("wqload", [128, 2 * D], BF16) as wqload:

        wc_prev = [None, None]
        wc_last = None
        for part in range(3):
            for i in range(DC):
                par = i % 2
                wq_s = wqload[:, par * D:(par + 1) * D]
                dep = [wc_prev[par]] if wc_prev[par] else []
                ldc = dma("gpsimd", wq_s, W_in[i * 128:(i + 1) * 128, part * D:(part + 1) * D], deps=dep)
                stc = dma("sync", win_s[i * 128:(i + 1) * 128, part * D:(part + 1) * D], wq_s, deps=[ldc])
                wc_prev[par] = stc
                wc_last = stc
        lastq = None
        wqload2 = stage[:, DC * E3:DC * E3 + 2 * D]
        q_guard = [None, None]
        for i in range(EC3):
            par = i % 2
            wq_s = wqload2[:, par * D:(par + 1) * D]
            dep = [q_guard[par]] if q_guard[par] else []
            ld = dma("gpsimd", wq_s, W_qkv[i * 128:(i + 1) * 128, :], deps=dep)
            lastq = transpose_blocks(
                lambda bb, wq_s=wq_s: wq_s[:, bb * 128:(bb + 1) * 128], DC,
                (lambda e, bb, pc, ps, i=i: e.copy(stage[:, bb * E3 + i * 128: bb * E3 + i * 128 + 128], ps[:, pc:pc + 128])),
                [ld])
            q_guard[par] = lastq
        stq = dma("sync", wqkvT_s.rearrange("(dc p) e -> p dc e", p=128), stage[:, 0:DC * E3], deps=[lastq])

        def wscale_pass(W_ap, ntiles, tilecols, ws_pp, wsi_pp, guard):
            z = P.op("vector", lambda e: e.memset(accv[:], 0.0), deps=[guard] if guard else [])
            prev = z
            slot_guard = [guard, guard]
            W_v = W_ap.rearrange("(a p) d -> p a d", p=128)
            chunks = []
            if tilecols >= 1024:
                for a in range(ntiles):
                    for d0 in range(0, tilecols, 1024):
                        chunks.append(W_v[:, a:a + 1, d0:d0 + 1024])
            else:
                ga = 1024 // tilecols
                for a0 in range(0, ntiles, ga):
                    chunks.append(W_v[:, a0:a0 + ga, :])
            for k, src in enumerate(chunks):
                par = k % 2
                w_s = wscan[:, par * 1024:par * 1024 + 1024]
                dep = [slot_guard[par]] if slot_guard[par] else []
                ld = dma("sync", w_s, src, deps=dep)
                rs = P.op("vector", lambda e, w_s=w_s: e.reduce_sum(colA[:, 0:1], w_s, AX, apply_absolute_value=True), deps=[ld, prev], fence=True)
                slot_guard[par] = rs
                prev = P.op("vector", lambda e: e.tensor_add(accv[:], accv[:], colA[:, 0:1]), deps=[rs], fence=True)
            dep0 = [prev, o_ones_f] + ([LP["psK"]] if LP["psK"] else [])
            m = P.op("tensor", lambda e: e.matmul(psK[0:1, 0:1], ones_col_f[:, 0:1], accv[:, 0:1], start=True, stop=True), deps=dep0)
            sc = P.op("vector", (lambda e, n=ntiles * 128 * tilecols: e.tensor_scalar(colA[0:1, 0:1], psK[0:1, 0:1], 1.0 / n, EPS_Q, OP.mult, OP.max)), deps=[m], fence=True)
            LP["psK"] = sc
            m2 = P.op("tensor", lambda e: e.matmul(psK[:, 0:1], ones_row_f[0:1, :], colA[0:1, 0:1], start=True, stop=True), deps=[sc, LP["psK"]])
            c1 = P.op("vector", lambda e: e.tensor_copy(ws_pp[:], psK[:, 0:1]), deps=[m2], fence=True)
            LP["psK"] = c1
            return P.op("vector", lambda e: e.reciprocal(wsi_pp[:], ws_pp[:]), deps=[c1], fence=True)

        ws1_done = wscale_pass(W1, H // 128, D, ws1_pp, ws1i_pp, None)
        ws2_done = wscale_pass(W2, D // 128, H, ws2_pp, ws2i_pp, ws1_done)
        bw1 = P.op("vector", lambda e: e.tensor_scalar_mul(b1w_row[:], b1_row[:], ws1i_pp[0:1, 0:1]), deps=[ws1_done, l_b1r])
        bw2 = P.op("vector", lambda e: e.tensor_scalar_mul(b2w_row[:], b2_row[:], ws2i_pp[0:1, 0:1]), deps=[ws2_done, l_b2r])

        def tern_build(W_ap, wsi, wsi_done, ntiles, tilecols, nblk, col_of, buf_guard, stage_guard):
            last_ev = None
            ld_guard = [buf_guard, buf_guard]
            tern_guard = [buf_guard, buf_guard]
            sg = stage_guard
            t_prev = None
            for i in range(ntiles):
                par = i % 2
                w_s = wload[:, par * H:par * H + tilecols]
                t_s = tern[:, par * H:par * H + tilecols]
                gd = [ld_guard[par]] if ld_guard[par] else []
                ld = dma("gpsimd", w_s, W_ap[i * 128:(i + 1) * 128, :], deps=gd)
                a = P.op("vector", lambda e, tc=tilecols, w_s=w_s: e.tensor_scalar(tA[:, 0:tc], w_s, wsi[:, 0:1], 0.5, OP.mult, OP.is_gt), deps=[ld, t_prev, wsi_done])
                b = P.op("vector", lambda e, tc=tilecols, w_s=w_s: e.tensor_scalar(tB[:, 0:tc], w_s, wsi[:, 0:1], -0.5, OP.mult, OP.is_lt), deps=[ld, t_prev, wsi_done])
                ld_guard[par] = b
                t = P.op("vector", lambda e, tc=tilecols, t_s=t_s: e.tensor_sub(t_s, tA[:, 0:tc], tB[:, 0:tc]), deps=[a, b, tern_guard[par]])
                t_prev = t
                last_ev = transpose_blocks(
                    lambda bb, t_s=t_s: t_s[:, bb * 128:(bb + 1) * 128], nblk,
                    (lambda e, bb, pc, ps, i=i: e.copy(stage[:, col_of(i, bb):col_of(i, bb) + 128], ps[:, pc:pc + 128])),
                    [t] + sg)
                sg = []
                tern_guard[par] = last_ev
            return last_ev

        e1 = tern_build(W1, ws1i_pp, ws1_done, H // 128, D, DC, lambda i, b: b * H + i * 128, None, [stq])
        st1 = dma("sync", t1_s.rearrange("(dc p) h -> p dc h", p=128), stage[:, 0:DC * H], deps=[e1])
        e2 = tern_build(W2, ws2i_pp, ws2_done, D // 128, H, HC, lambda i, b: b * D + i * 128, e1, [st1])
        st2_ = dma("sync", t2_s.rearrange("(hc p) d -> p hc d", p=128), stage[:, 0:HC * D], deps=[e2])

    prep_done = [st1, st2_, stq, bw1, bw2, wc_last, l_bin, l_sw, l_sb, l_iw, l_ib,
                 l_bqk, l_bo, l_bv, l_id, l_ca, ws1_done, ws2_done]

    def ln_transpose_all(src_ap, start_deps, xlnT, tile_deps=None):
        slot_guard = [None, None]
        trs = []
        for i in range(NT):
            par = i % 2
            xl_s = xlnt[:, par * D:(par + 1) * D]
            ng = bool(tile_deps)
            td = [tile_deps[i]] if tile_deps else list(start_deps)
            sl, ld = load_x_tile(ring_in, xring, i, src_ap, td, nogate=ng)
            ap = ln_tile(xring[:, sl * D:(sl + 1) * D], xl_s,
                         [ld] + ([slot_guard[par]] if slot_guard[par] else []), par, nogate=ng)
            ring_in.use(sl, ap)
            last = transpose_blocks(
                lambda b, xl_s=xl_s: xl_s[:, b * 128:(b + 1) * 128], DC,
                (lambda e, b, pc, ps, i=i: e.copy(xlnT[:, b * S + i * 128: b * S + i * 128 + 128], ps[:, pc:pc + 128])),
                [ap])
            slot_guard[par] = last
            trs.append(last)
        return trs

    def sublayer_attn_a(src_ap, dst_ap, start_deps, tile_deps=None):
        stores = []
        with rsb("xlnTa", [128, DC * S], BF16) as xlnT, \
             rsb("ybuf", [128, DC * S], BF16) as ybuf:
          with rsb("xpuc", [128, EC3 * 512], BF16) as xpuc, \
             rsb("kvb", [128, DC * 512], BF16) as kvb, \
             rsb("winb", [128, 2 * 4 * D], BF16) as winb, \
             rsb("diag3", [128, EC3 * 3 * 128], BF16) as diag3, \
             rsb("diag4", [128, DC * 4 * 128], BF16) as diag4:
            trs = ln_transpose_all(src_ap, start_deps, xlnT, tile_deps)
            tr_last = trs[-1]
            dg_last = None
            for c in range(EC3):
                for k in range(3):
                    dg_last = P.op("vector", (lambda e, c=c, k=k: e.tensor_scalar_mul(
                        diag3[:, (c * 3 + k) * 128:(c * 3 + k + 1) * 128], identt[:],
                        short_w_pp[:, c * 3 + k:c * 3 + k + 1])), deps=[l_id, l_sw])
            for c in range(DC):
                for k in range(4):
                    dg_last = P.op("vector", (lambda e, c=c, k=k: e.tensor_scalar_mul(
                        diag4[:, (c * 4 + k) * 128:(c * 4 + k + 1) * 128], identt[:],
                        inner_w_pp[:, c * 4 + k:c * 4 + k + 1])), deps=[l_id, l_iw])

            y_last = None
            win_prev = [None, None]
            prev_window_v = None
            for (o0, L) in WINDOWS:
                halo = 0 if o0 == 0 else 5
                xp0 = o0 - halo
                XL = L + halo
                uh = 0 if o0 == 0 else 3
                UL = L + uh
                ub = halo - uh
                ev_xp = []
                WG = 4
                state_lw = None
                for c in range(EC3):
                    ps, psn = (psA, "psA") if c % 2 == 0 else (psB, "psB")
                    wb = (c // WG) % 2
                    ci = c % WG
                    if ci == 0:
                        state_lw = dma("sync", winb[:, wb * WG * D:(wb + 1) * WG * D],
                                       win_s.rearrange("(dc p) e -> p dc e", p=128)[:, :, c * 128:(c + WG) * 128],
                                       deps=[wc_last] + ([win_prev[wb]] if win_prev[wb] else []))
                    lw = state_lw
                    tr_need = trs[min((o0 + L - 1) // 128, NT - 1)]
                    dep0 = [lw, tr_need, prev_window_v] + ([LP[psn]] if LP[psn] else [])
                    mlast = None
                    for dcc in range(DC):
                        mlast = P.op("tensor", (lambda e, dcc=dcc, ps=ps, wb=wb, ci=ci, xp0=xp0, XL=XL: e.matmul(
                            ps[:, 0:XL], winb[:, wb * WG * D + dcc * WG * 128 + ci * 128: wb * WG * D + dcc * WG * 128 + (ci + 1) * 128],
                            xlnT[:, dcc * S + xp0: dcc * S + xp0 + XL],
                            start=(dcc == 0), stop=(dcc == DC - 1))), deps=(dep0 if dcc == 0 else []))
                    if ci == WG - 1:
                        win_prev[wb] = mlast
                    ev = P.op("scalar", (lambda e, c=c, ps=ps, XL=XL: e.activation(
                        xpuc[:, c * 512:c * 512 + XL], ps[:, 0:XL], AF.Identity,
                        bias=b_in_pp[:, c:c + 1])), deps=[mlast, l_bin, prev_window_v])
                    LP[psn] = ev
                    ev_xp.append(ev)
                ev_uc = []
                for c in range(EC3):
                    ps, psn = (psC, "psC") if c % 2 == 0 else (psD, "psD")
                    dep0 = [ev_xp[c], dg_last] + ([LP[psn]] if LP[psn] else [])
                    mlast = None
                    first = True
                    for dlt in (0, 1, 2):
                        kk = 2 - dlt
                        if o0 == 0:
                            lo, src0, n = dlt, 0, UL - dlt
                        else:
                            lo, src0, n = 0, ub - dlt + 0, UL
                        if n <= 0:
                            continue
                        mlast = P.op("tensor", (lambda e, c=c, kk=kk, ps=ps, lo=lo, src0=src0, n=n, first=first, UL=UL: e.matmul(
                            ps[:, lo:lo + n],
                            diag3[:, (c * 3 + kk) * 128:(c * 3 + kk + 1) * 128],
                            xpuc[:, c * 512 + src0: c * 512 + src0 + n],
                            start=first, stop=(kk == 0))), deps=(dep0 if first else []))
                        first = False
                    ev = P.op("scalar", (lambda e, c=c, ps=ps, UL=UL, ub=ub: e.activation(
                        xpuc[:, c * 512 + ub: c * 512 + ub + UL], ps[:, 0:UL], AF.Identity,
                        bias=short_b_pp[:, c:c + 1])), deps=[mlast, l_sb])
                    LP[psn] = ev
                    ev_uc.append(ev)
                ev_kv = []
                for c in range(DC):
                    ev = P.op("vector", (lambda e, c=c, ub=ub, UL=UL: e.tensor_mul(
                        kvb[:, c * 512: c * 512 + UL],
                        xpuc[:, (8 + c) * 512 + ub: (8 + c) * 512 + ub + UL],
                        xpuc[:, (16 + c) * 512 + ub: (16 + c) * 512 + ub + UL])),
                        deps=[ev_uc[8 + c], ev_uc[16 + c]])
                    ev_kv.append(ev)
                ev_vi = []
                for c in range(DC):
                    ps, psn = (psC, "psC") if c % 2 == 0 else (psD, "psD")
                    dep0 = [ev_kv[c]] + ([LP[psn]] if LP[psn] else [])
                    mlast = None
                    first = True
                    for dlt in (0, 1, 2, 3):
                        kk = 3 - dlt
                        if o0 == 0:
                            lo, src0, n = dlt, 0, L - dlt
                        else:
                            lo, src0, n = 0, uh - dlt, L
                        if n <= 0:
                            continue
                        mlast = P.op("tensor", (lambda e, c=c, kk=kk, ps=ps, lo=lo, src0=src0, n=n, first=first: e.matmul(
                            ps[:, lo:lo + n],
                            diag4[:, (c * 4 + kk) * 128:(c * 4 + kk + 1) * 128],
                            kvb[:, c * 512 + src0: c * 512 + src0 + n],
                            start=first, stop=(kk == 0))), deps=(dep0 if first else []))
                        first = False
                    ev = P.op("scalar", (lambda e, c=c, ps=ps, L=L, uh=uh: e.activation(
                        kvb[:, c * 512 + uh: c * 512 + uh + L], ps[:, 0:L], AF.Identity,
                        bias=inner_b_pp[:, c:c + 1])), deps=[mlast, l_ib])
                    LP[psn] = ev
                    ev_vi.append(ev)
                for c in range(DC):
                    y_last = P.op("vector", (lambda e, c=c, o0=o0, L=L, uh=uh, ub=ub: e.tensor_mul(
                        ybuf[:, c * S + o0: c * S + o0 + L],
                        xpuc[:, c * 512 + ub + uh: c * 512 + ub + uh + L],
                        kvb[:, c * 512 + uh: c * 512 + uh + L])),
                        deps=[ev_vi[c], ev_uc[c]])
                prev_window_v = y_last
          with rsb("woutb", [128, DC * D], BF16) as woutb:
            lwo = dma("gpsimd", woutb[:], W_out.rearrange("(dc p) e -> p dc e", p=128), deps=[y_last])
            for i in range(NT):
                sl2, ld2 = load_x_tile(ring_x2, xring2, i, src_ap)
                so, prevo = ring_o.slot(i)
                evs = []
                for eh in range(2):
                    ps, psn = (psA, "psA") if eh == 0 else (psB, "psB")
                    dep0 = [lwo, l_bo, o_ones_b, y_last] + ([LP[psn]] if LP[psn] else [])
                    mlast = None
                    for dcc in range(DC):
                        mlast = P.op("tensor", (lambda e, dcc=dcc, i=i, eh=eh, ps=ps: e.matmul(
                            ps[:, 0:512], ybuf[:, dcc * S + i * 128: dcc * S + (i + 1) * 128],
                            woutb[:, dcc * D + eh * 512: dcc * D + eh * 512 + 512],
                            start=(dcc == 0), stop=False)), deps=(dep0 if dcc == 0 else []))
                    mlast = P.op("tensor", (lambda e, eh=eh, ps=ps: e.matmul(
                        ps[:, 0:512], ones_row[0:1, :], bout_row[0:1, eh * 512:eh * 512 + 512],
                        start=False, stop=True)), deps=[])
                    ev = P.op("vector", (lambda e, eh=eh, ps=ps, sl2=sl2, so=so: e.tensor_add(
                        oring[:, so * D + eh * 512: so * D + eh * 512 + 512], ps[:, 0:512],
                        xring2[:, sl2 * D + eh * 512: sl2 * D + eh * 512 + 512])),
                        deps=[mlast, ld2] + prevo)
                    LP[psn] = ev
                    ring_x2.use(sl2, ev)
                    evs.append(ev)
                stv = dma("sync", dst_ap[i * 128:(i + 1) * 128, :], oring[:, so * D:(so + 1) * D], deps=evs)
                ring_o.use(so, stv)
                stores.append(stv)
        return stores

    def sublayer_ffn(src_ap, dst_ap, start_deps, tile_deps=None):
        stores = []
        stores_A = []
        with rsb("t1b", [128, DC * H], BF16) as t1b, \
             rsb("xlnf2", [128, 2 * D], F32) as xlnf2, \
             rsb("kiA", [128, D], I16) as kiA, \
             rsb("xib", [128, D], BF16) as xib, \
             rsb("xiT", [128, DC * 128], BF16) as xiT, \
             rsb("hB", [128, 2 * H], F32) as hB2, \
             rsb("kiH", [128, H], I16) as kiH, \
             rsb("hib", [128, H], BF16) as hib, \
             rsb("hqb", [128, H], BF16) as hqb, \
             rsb("hstage", [128, H], BF16) as hstage:
            lt1s = []
            for k4 in range(4):
                lt1s.append(dma("sync", t1b.ap().rearrange("p (dc h) -> p dc h", dc=DC)[:, :, k4 * 1024:(k4 + 1) * 1024],
                                t1_s.rearrange("(dc p) h -> p dc h", p=128)[:, :, k4 * 1024:(k4 + 1) * 1024],
                                deps=[st1] + list(start_deps)))
            state = {}
            pv = None
            for i in range(NT):
                par = i % 2
                o6 = par * 12
                td = [tile_deps[i]] if tile_deps else list(start_deps)
                sl, ld = load_x_tile(ring_in, xring, i, src_ap, td, nogate=bool(tile_deps))
                xt = xring[:, sl * D:(sl + 1) * D]
                a0 = P.op("vector", lambda e, xt=xt, o6=o6: e.bn_stats(st6[:, o6:o6 + 6], xt[:, 0:512]), deps=[ld, pv], fence=True, nogate=True)
                a1 = P.op("vector", lambda e, xt=xt, o6=o6: e.bn_stats(st6[:, o6 + 6:o6 + 12], xt[:, 512:1024]), deps=[ld, a0], fence=True, nogate=True)
                pv = P.op("vector", lambda e, i=i, o6=o6: e.bn_aggr(mvbuf[:, 2 * i:2 * i + 2], st6[:, o6:o6 + 12]), deps=[a0, a1], fence=True, nogate=True)
                ring_in.use(sl, pv)
            b0 = P.op("vector", lambda e: e.tensor_single_scalar(rstdb[:], mvbuf.ap().rearrange("p (t two) -> p t two", two=2)[:, :, 1], EPS_LN, OP.add), deps=[pv], fence=True, nogate=True)
            b1_ = P.op("scalar", lambda e: e.activation(rstdb[:], rstdb[:], AF.Ln), deps=[b0], fence=True, nogate=True)
            b2_ = P.op("scalar", lambda e: e.activation(rstdb[:], rstdb[:], AF.Exp, scale=-0.5), deps=[b1_], fence=True, nogate=True)
            b3_ = P.op("vector", lambda e: e.scalar_tensor_tensor(negmb[:], mvbuf.ap().rearrange("p (t two) -> p t two", two=2)[:, :, 0], -1.0, rstdb[:], OP.mult, OP.mult), deps=[b2_], fence=True, nogate=True)

            r5p = {}
            hBguard = {}
            sth_guard = {}

            def FRONT_a(i):
                par = i % 2
                pA, pB, pC = colA[:, par:par + 1], colB[:, par:par + 1], colC[:, par:par + 1]
                xlnf_s = xlnf2[:, par * D:(par + 1) * D]
                sl, ld = load_x_tile(ring_in, xring, i, src_ap, start_deps)
                ap = P.op("scalar", lambda e, i=i, xt=xring[:, sl * D:(sl + 1) * D], xs=xlnf_s: e.activation(
                    xs, xt, AF.Identity, bias=negmb[:, i:i + 1], scale=rstdb[:, i:i + 1]), deps=[ld, b3_, r5p.get(par)])
                ring_in.use(sl, ap)
                q1 = P.op("vector", lambda e, xs=xlnf_s, pA=pA: e.reduce_max(pA, xs, AX, apply_absolute_value=True), deps=[ap], fence=True)
                q2 = P.op("vector", lambda e, pA=pA: e.tensor_single_scalar(pA, pA, EPS_Q, OP.max), deps=[q1], fence=True)
                q3 = P.op("vector", lambda e, pA=pA, pB=pB: e.reciprocal(pB, pA), deps=[q2], fence=True)
                q4 = P.op("vector", lambda e, pB=pB: e.tensor_single_scalar(pB, pB, QB, OP.mult), deps=[q3], fence=True)
                q5 = P.op("vector", lambda e, pA=pA, pC=pC: e.tensor_single_scalar(pC, pA, 1.0 / QB, OP.mult), deps=[q2], fence=True)
                state[("ap", i)] = ap
                state[("q4", i)] = q4
                state[("q5", i)] = q5

            def FRONT_b(i):
                par = i % 2
                pB, pC = colB[:, par:par + 1], colC[:, par:par + 1]
                xlnf_s = xlnf2[:, par * D:(par + 1) * D]
                ap, q4, q5 = state.pop(("ap", i)), state.pop(("q4", i)), state.pop(("q5", i))
                uop = P.op("scalar", lambda e, xs=xlnf_s, pB=pB: e.activation(xs, xs, AF.Identity, bias=half_col[:, 0:1], scale=pB), deps=[q4, ap, o_half, r5p.get(par)], fence=True)
                state[("uop", par)] = uop
                r2 = P.op("vector", lambda e, xs=xlnf_s: e.tensor_scalar(kiA[:], xs, MAGIC, MAGIC, OP.add, OP.subtract), deps=[uop])
                r3 = P.op("vector", lambda e, xs=xlnf_s: e.scalar_tensor_tensor(xib[:], kiA[:], 0.0, xs, OP.add, OP.is_gt), deps=[r2, state.get("xib_rd")])
                r4 = P.op("vector", lambda e, xs=xlnf_s: e.scalar_tensor_tensor(xs, kiA[:], 0.0, xib[:], OP.add, OP.subtract), deps=[r3])
                r5 = P.op("vector", lambda e, pC=pC, xs=xlnf_s: e.tensor_scalar_mul(xib[:], xs, pC), deps=[r4, q5])
                r5p[par] = r5
                tr = transpose_blocks(lambda b: xib[:, b * 128:(b + 1) * 128], DC,
                                      (lambda e, j, hi, ps: e.copy(xiT[:, j * 128:hi * 128], ps[:, 0:(hi - j) * 128])),
                                      [r5], batch=True)
                state["xib_rd"] = tr
                state[("tr", i)] = tr

            def MM1GEL(i):
                par = i % 2
                hs = hB2[:, par * H:(par + 1) * H]
                gel = []
                for hc2 in range(H // 512):
                    ps, psn = (psA, "psA") if hc2 % 2 == 0 else (psB, "psB")
                    dep0 = [state[("tr", i)], lt1s[hc2 // 2]] + ([LP[psn]] if LP[psn] else [])
                    for dcc in range(DC):
                        P.op("tensor", (lambda e, dcc=dcc, hc2=hc2, ps=ps: e.matmul(
                            ps[:, 0:512], xiT[:, dcc * 128:(dcc + 1) * 128],
                            t1b[:, dcc * H + hc2 * 512: dcc * H + hc2 * 512 + 512],
                            start=(dcc == 0), stop=False)), deps=(dep0 if dcc == 0 else []))
                    mlast = P.op("tensor", (lambda e, hc2=hc2, ps=ps: e.matmul(
                        ps[:, 0:512], ones_row[0:1, :], b1w_row[0:1, hc2 * 512: hc2 * 512 + 512],
                        start=False, stop=True)), deps=[])
                    ev = P.op("scalar", (lambda e, hc2=hc2, ps=ps, hs=hs: e.activation(
                        hs[:, hc2 * 512:hc2 * 512 + 512], ps[:, 0:512], AF.Gelu,
                        scale=ws1_pp[:, 0:1])), deps=[mlast, hBguard.get(par)])
                    LP[psn] = ev
                    gel.append(ev)
                state[("gel", i)] = gel

            def BACKQV_h(i):
                par = i % 2
                pA, pB, pC = colA[:, par:par + 1], colB[:, par:par + 1], colC[:, par:par + 1]
                hs = hB2[:, par * H:(par + 1) * H]
                gel = state[("gel", i)]
                h1 = P.op("vector", lambda e, pA=pA, hs=hs: e.reduce_max(pA, hs, AX, apply_absolute_value=True), deps=gel, fence=True)
                h2 = P.op("vector", lambda e, pA=pA: e.tensor_single_scalar(pA, pA, EPS_Q, OP.max), deps=[h1], fence=True)
                h3 = P.op("vector", lambda e, pA=pA, pB=pB: e.reciprocal(pB, pA), deps=[h2], fence=True)
                h4 = P.op("vector", lambda e, pB=pB: e.tensor_single_scalar(pB, pB, QB, OP.mult), deps=[h3, state.get(("uop", par))], fence=True)
                h5 = P.op("vector", lambda e, pA=pA, pC=pC: e.tensor_single_scalar(pC, pA, 1.0 / QB, OP.mult), deps=[h2], fence=True)
                state[("h45", i)] = (h4, h5)

            def BACKQV_u(i):
                par = i % 2
                pB = colB[:, par:par + 1]
                hs = hB2[:, par * H:(par + 1) * H]
                gel = state.pop(("gel", i))
                h4, h5 = state[("h45", i)]
                uh = P.op("scalar", lambda e, pB=pB, hs=hs: e.activation(hs, hs, AF.Identity, bias=half_col[:, 0:1], scale=pB), deps=[h4, o_half] + gel, fence=True)
                k2 = P.op("vector", lambda e, hs=hs: e.tensor_scalar(kiH[:], hs, MAGIC, MAGIC, OP.add, OP.subtract), deps=[uh])
                state[("k2", i)] = k2

            def BACKQV_g(i):
                par = i % 2
                pC = colC[:, par:par + 1]
                hs = hB2[:, par * H:(par + 1) * H]
                k2 = state.pop(("k2", i))
                h4, h5 = state.pop(("h45", i))
                g2 = P.op("vector", lambda e, hs=hs: e.scalar_tensor_tensor(hib[:], kiH[:], 0.0, hs, OP.add, OP.is_gt), deps=[k2, state.get("hq_rd")])
                hBguard[par] = g2
                kq = P.op("vector", lambda e, pC=pC: e.tensor_scalar_mul(hqb[:], kiH[:], pC), deps=[g2, h5, state.get("hq_rd")])
                gq = P.op("vector", lambda e, pC=pC: e.tensor_scalar(hib[:], hib[:], pC, -1.0, OP.mult, OP.mult), deps=[g2, h5])
                state[("kqgq", i)] = [kq, gq]

            def HTRANS(i):
                kqgq = state.pop(("kqgq", i))
                hz = None
                for k in range(HC // 4):
                    j, hi = k * 4, k * 4 + 4
                    ps, psn = (psT, "psT") if k % 2 == 0 else (psU, "psU")
                    tdep = kqgq + [_lid[0]] + ([LP[psn]] if LP[psn] else [])
                    tlast = None
                    for b in range(j, hi):
                        P.op("tensor", (lambda e, b=b, j=j, ps=ps: e.matmul(
                            ps[:, (b - j) * 128:(b - j + 1) * 128],
                            hqb[:, b * 128:(b + 1) * 128], identt[:],
                            start=True, stop=False)), deps=(tdep if b == j else []))
                        tlast = P.op("tensor", (lambda e, b=b, j=j, ps=ps: e.matmul(
                            ps[:, (b - j) * 128:(b - j + 1) * 128],
                            hib[:, b * 128:(b + 1) * 128], identt[:],
                            start=False, stop=True)), deps=[])
                    gd = [sth_guard[k]] if sth_guard.get(k) else []
                    ev = P.op("scalar", (lambda e, j=j, hi=hi, ps=ps: e.copy(
                        hstage[:, j * 128:hi * 128], ps[:, 0:(hi - j) * 128])), deps=[tlast] + gd)
                    LP[psn] = ev
                    sth = dma("sync", hiT_s[i, :, j * 128:hi * 128], hstage[:, j * 128:hi * 128], deps=[ev])
                    sth_guard[k] = sth
                    stores_A.append(sth)
                    hz = ev
                state["hq_rd"] = hz

            for i in range(NT):
                if i >= 1:
                    BACKQV_h(i - 1)
                FRONT_a(i)
                if i >= 1:
                    BACKQV_u(i - 1)
                FRONT_b(i)
                if i >= 1:
                    BACKQV_g(i - 1)
                MM1GEL(i)
                if i >= 1:
                    HTRANS(i - 1)
            BACKQV_h(NT - 1)
            BACKQV_u(NT - 1)
            BACKQV_g(NT - 1)
            HTRANS(NT - 1)
        with rsb("t2b", [128, HC * D], BF16) as t2b, \
             rsb("hiTb", [128, 3 * H], BF16) as hiTb:
            lt2s = []
            for k4 in range(4):
                lt2s.append(dma("sync", t2b.ap().rearrange("p (hc d) -> p hc d", hc=HC)[:, k4 * 8:(k4 + 1) * 8, :],
                                t2_s.rearrange("(hc p) d -> p hc d", p=128)[:, k4 * 8:(k4 + 1) * 8, :],
                                deps=[st2_, stores_A[-1]]))
            ringB = Ring(3)
            for i in range(NT):
                slh, prevh = ringB.slot(i)
                ldh = dma("sync", hiTb[:, slh * H:(slh + 1) * H], hiT_s[i, :, :], deps=[stores_A[-1]] + prevh)
                sl2, ld2 = load_x_tile(ring_x2, xring2, i, src_ap)
                so, prevo = ring_o.slot(i)
                evs = []
                for eh in range(2):
                    ps, psn = (psC, "psC") if eh == 0 else (psD, "psD")
                    dep0 = [ldh, lt2s[0]] + ([LP[psn]] if LP[psn] else [])
                    for hcc in range(HC):
                        dd = dep0 if hcc == 0 else ([lt2s[hcc // 8]] if hcc % 8 == 0 else [])
                        P.op("tensor", (lambda e, hcc=hcc, eh=eh, ps=ps, slh=slh: e.matmul(
                            ps[:, 0:512], hiTb[:, slh * H + hcc * 128: slh * H + (hcc + 1) * 128],
                            t2b[:, hcc * D + eh * 512: hcc * D + eh * 512 + 512],
                            start=(hcc == 0), stop=False)), deps=dd)
                    mlast = P.op("tensor", (lambda e, eh=eh, ps=ps: e.matmul(
                        ps[:, 0:512], ones_row[0:1, :], b2w_row[0:1, eh * 512: eh * 512 + 512],
                        start=False, stop=True)), deps=[])
                    ringB.use(slh, mlast)
                    ev = P.op("vector", (lambda e, eh=eh, ps=ps, so=so, sl2=sl2: e.scalar_tensor_tensor(
                        oring[:, so * D + eh * 512: so * D + eh * 512 + 512], ps[:, 0:512],
                        ws2_pp[:, 0:1], xring2[:, sl2 * D + eh * 512: sl2 * D + eh * 512 + 512],
                        OP.mult, OP.add)), deps=[mlast, ld2] + prevo)
                    LP[psn] = ev
                    ring_x2.use(sl2, ev)
                    evs.append(ev)
                stv = dma("sync", dst_ap[i * 128:(i + 1) * 128, :], oring[:, so * D:(so + 1) * D], deps=evs)
                ring_o.use(so, stv)
                stores.append(stv)
        return stores

    def sublayer_attn_b(src_ap, dst_ap, start_deps, tile_deps=None):
        stores = []
        with rsb("qb", [128, DC * S], BF16) as qb, \
             rsb("kb", [128, DC * S], BF16) as kb, \
             rsb("vb", [128, NT * D], BF16) as vb:
          with rsb("xlnTb", [128, DC * S], BF16) as xlnT, \
             rsb("wqb", [128, 2 * D], BF16) as wqb, \
             rsb("wvb", [128, DC * 512], BF16) as wvb:
            trs = ln_transpose_all(src_ap, start_deps, xlnT, tile_deps)
            tr_last = trs[-1]
            qk_last = None
            wq_prev = [None, None]
            for part in range(2):
                dstb = qb if part == 0 else kb
                for c in range(DC):
                    wslot = c % 2
                    lw = dma("sync", wqb[:, wslot * D:(wslot + 1) * D],
                             wqkvT_s.rearrange("(dc p) e -> p dc e", p=128)[:, :, part * D + c * 128: part * D + (c + 1) * 128],
                             deps=[stq] + ([wq_prev[wslot]] if wq_prev[wslot] else []))
                    for tch in range(4):
                        ps, psn = (psA, "psA") if tch % 2 == 0 else (psB, "psB")
                        dep0 = [lw, trs[4 * tch + 3]] + ([LP[psn]] if LP[psn] else [])
                        mlast = None
                        for dcc in range(DC):
                            mlast = P.op("tensor", (lambda e, dcc=dcc, tch=tch, ps=ps, wslot=wslot: e.matmul(
                                ps[:, 0:512], wqb[:, wslot * D + dcc * 128: wslot * D + (dcc + 1) * 128],
                                xlnT[:, dcc * S + tch * 512: dcc * S + tch * 512 + 512],
                                start=(dcc == 0), stop=(dcc == DC - 1))), deps=(dep0 if dcc == 0 else []))
                        wq_prev[wslot] = mlast
                        ev = P.op("scalar", (lambda e, c=c, tch=tch, ps=ps, part=part, dstb=dstb: e.activation(
                            dstb[:, c * S + tch * 512: c * S + tch * 512 + 512], ps[:, 0:512],
                            AF.Identity, bias=bq_pp[:, part * DC + c: part * DC + c + 1])), deps=[mlast, l_bqk])
                        LP[psn] = ev
                        qk_last = ev
            v_last = None
            lwv_prev = wq_prev[0]
            for eh in range(2):
                lwv = dma("sync", wvb[:],
                          wqkvT_s.rearrange("(dc p) e -> p dc e", p=128)[:, :, 2 * D + eh * 512: 2 * D + eh * 512 + 512],
                          deps=[stq, lwv_prev, wq_prev[1]])
                for i in range(NT):
                    ps, psn = (psC, "psC") if i % 2 == 0 else (psD, "psD")
                    dep0 = [lwv, trs[i], o_ones_b, l_bv] + ([LP[psn]] if LP[psn] else [])
                    mlast = None
                    for dcc in range(DC):
                        mlast = P.op("tensor", (lambda e, dcc=dcc, i=i, eh=eh, ps=ps: e.matmul(
                            ps[:, 0:512], xlnT[:, dcc * S + i * 128: dcc * S + (i + 1) * 128],
                            wvb[:, dcc * 512: dcc * 512 + 512],
                            start=(dcc == 0), stop=False)), deps=(dep0 if dcc == 0 else []))
                    mlast = P.op("tensor", (lambda e, eh=eh, ps=ps: e.matmul(
                        ps[:, 0:512], ones_row[0:1, :], bv_row[0:1, eh * 512:eh * 512 + 512],
                        start=False, stop=True)), deps=[])
                    ev = P.op("vector", (lambda e, i=i, eh=eh, ps=ps: e.tensor_copy(
                        vb[:, i * D + eh * 512: i * D + eh * 512 + 512], ps[:, 0:512])), deps=[mlast])
                    LP[psn] = ev
                    v_last = ev
                lwv_prev = mlast
          with rsb("scoreb", [128, 2 * S], F32) as scoreb2, \
             rsb("probb", [128, S], BF16) as probb, \
             rsb("attT", [128, S], BF16) as attT:
            st8 = {}

            def SCORES(qi):
                nk = (qi + 1) * 128
                par = qi % 2
                scoreb = scoreb2[:, par * S:(par + 1) * S]
                sl2, ld2 = load_x_tile(ring_x2, xring2, qi, src_ap)
                so, prevo = ring_o.slot(qi)
                guard = st8.get(("m1", qi - 2))
                ev_sc = []
                for kb0 in range(0, nk, 512):
                    w = min(512, nk - kb0)
                    ps, psn = (psA, "psA") if (kb0 // 512) % 2 == 0 else (psB, "psB")
                    dep0 = [qk_last, v_last] + ([LP[psn]] if LP[psn] else [])
                    mlast = None
                    for dcc in range(DC):
                        mlast = P.op("tensor", (lambda e, dcc=dcc, qi=qi, kb0=kb0, w=w, ps=ps: e.matmul(
                            ps[:, 0:w], qb[:, dcc * S + qi * 128: dcc * S + (qi + 1) * 128],
                            kb[:, dcc * S + kb0: dcc * S + kb0 + w],
                            start=(dcc == 0), stop=(dcc == DC - 1))), deps=(dep0 if dcc == 0 else []))
                    dlo = qi * 128 - kb0
                    if 0 <= dlo < w:
                        evp = None
                        if dlo > 0:
                            evp = P.op("scalar", (lambda e, kb0=kb0, dlo=dlo, ps=ps, scoreb=scoreb: e.copy(
                                scoreb[:, kb0:kb0 + dlo], ps[:, 0:dlo])), deps=[mlast, guard])
                            ev_sc.append(evp)
                        evd = P.op("vector", (lambda e, kb0=kb0, dlo=dlo, ps=ps, scoreb=scoreb: e.scalar_tensor_tensor(
                            scoreb[:, kb0 + dlo:kb0 + dlo + 128], ps[:, dlo:dlo + 128], 0.0,
                            causalt[:], OP.add, OP.add)), deps=[mlast, l_ca, guard, evp])
                        ev_sc.append(evd)
                        ev = evd
                    else:
                        ev = P.op("scalar", (lambda e, kb0=kb0, w=w, ps=ps, scoreb=scoreb: e.copy(
                            scoreb[:, kb0:kb0 + w], ps[:, 0:w])), deps=[mlast, guard])
                        ev_sc.append(ev)
                    LP[psn] = ev
                st8[("evsc", qi)] = ev_sc
                st8[("xo", qi)] = (sl2, ld2, so, prevo)

            def SM(qi):
                nk = (qi + 1) * 128
                par = qi % 2
                scoreb = scoreb2[:, par * S:(par + 1) * S]
                pA, pB, pC, pD = (colA[:, par:par + 1], colB[:, par:par + 1],
                                  colC[:, par:par + 1], colD[:, par:par + 1])
                ev_sc = st8.pop(("evsc", qi))
                m1 = P.op("vector", (lambda e, nk=nk, pA=pA, scoreb=scoreb: e.reduce_max(pA, scoreb[:, 0:nk], AX)), deps=ev_sc, fence=True)
                st8[("m1", qi)] = m1
                m2 = P.op("vector", (lambda e, pA=pA, pB=pB: e.tensor_single_scalar(pB, pA, -1.0 / 32.0, OP.mult)), deps=[m1], fence=True)
                ex = P.op("scalar", (lambda e, nk=nk, pB=pB, pC=pC, scoreb=scoreb: e.activation(
                    probb[:, 0:nk], scoreb[:, 0:nk], AF.Exp, bias=pB,
                    scale=float(1.0 / 32.0), accum_out=pC)), deps=[m2, st8.get(("avT", qi - 1))], fence=True)
                rc = P.op("vector", (lambda e, pC=pC, pD=pD: e.reciprocal(pD, pC)), deps=[ex], fence=True)
                st8[("sm", qi)] = (ex, rc, pD)

            def TRANSAV(qi):
                ex, rc, pD = st8.pop(("sm", qi))
                sl2, ld2, so, prevo = st8.pop(("xo", qi))
                trp = transpose_blocks(lambda b: probb[:, b * 128:(b + 1) * 128], qi + 1,
                                       (lambda e, b, pc, ps: e.copy(attT[:, b * 128:(b + 1) * 128], ps[:, pc:pc + 128])),
                                       [ex])
                evs = []
                mlast = None
                for eh in range(2):
                    ps, psn = (psC, "psC") if eh == 0 else (psD, "psD")
                    dep0 = [trp, v_last] + ([LP[psn]] if LP[psn] else [])
                    for kt in range(qi + 1):
                        mlast = P.op("tensor", (lambda e, kt=kt, eh=eh, ps=ps, qi=qi: e.matmul(
                            ps[:, 0:512], attT[:, kt * 128:(kt + 1) * 128],
                            vb[:, kt * D + eh * 512: kt * D + eh * 512 + 512],
                            start=(kt == 0), stop=(kt == qi))), deps=(dep0 if kt == 0 else []))
                    ev = P.op("vector", (lambda e, eh=eh, ps=ps, so=so, sl2=sl2, pD=pD: e.scalar_tensor_tensor(
                        oring[:, so * D + eh * 512: so * D + eh * 512 + 512], ps[:, 0:512],
                        pD, xring2[:, sl2 * D + eh * 512: sl2 * D + eh * 512 + 512],
                        OP.mult, OP.add)), deps=[mlast, rc, ld2] + prevo)
                    LP[psn] = ev
                    ring_x2.use(sl2, ev)
                    evs.append(ev)
                st8[("avT", qi)] = mlast
                stv = dma("sync", dst_ap[qi * 128:(qi + 1) * 128, :], oring[:, so * D:(so + 1) * D], deps=evs)
                ring_o.use(so, stv)
                stores.append(stv)

            for qi in range(NT):
                SCORES(qi)
                if qi >= 1:
                    SM(qi - 1)
                    TRANSAV(qi - 1)
            SM(NT - 1)
            TRANSAV(NT - 1)
        return stores

    fns = [sublayer_attn_a, sublayer_ffn, sublayer_attn_b, sublayer_ffn][:n_sub]
    last_stores = prep_done
    first_phase = True
    for s in range(NSEQ):
        cur_src = x_in[s, :, :]
        scrs = [xscr_a, xscr_b]
        for k, fn in enumerate(fns):
            dst = y_out[s, :, :] if k == len(fns) - 1 else scrs[k % 2]
            gate = list(last_stores) if first_phase else last_stores[-1:]
            tds = None if first_phase or len(last_stores) != NT else list(last_stores)
            first_phase = False
            P.set_gate(gate)
            done = fn(cur_src, dst, gate, tile_deps=tds)
            last_stores = done
            cur_src = dst
    P.set_gate([])
    P.finish(last_stores[-2:])
    P.emit(block)
    ctx.close()
    return nc


def kernel(**inputs):
    xfull = np.ascontiguousarray(inputs["L0_attn_a_u"])
    weights = {
        "W_in": inputs["L0_attn_a_W_in"], "b_in": inputs["L0_attn_a_b_in"],
        "short_w": inputs["L0_attn_a_short_w"], "short_b": inputs["L0_attn_a_short_b"],
        "inner_w": inputs["L0_attn_a_inner_w"], "inner_b": inputs["L0_attn_a_inner_b"],
        "W_out": inputs["L0_attn_a_W_out"], "b_out": inputs["L0_attn_a_b_out"],
        "W_qkv": inputs["L0_attn_b_W_qkv"], "b_qkv": inputs["L0_attn_b_b_qkv"],
        "W1": inputs["L0_ffn_W1"], "b1": inputs["L0_ffn_b1"],
        "W2": inputs["L0_ffn_W2"], "b2": inputs["L0_ffn_b2"],
    }
    weights = {k: np.ascontiguousarray(v) for k, v in weights.items()}
    ident_np = np.eye(128, dtype=ml_dtypes.bfloat16)
    causal_np = np.where(np.tril(np.ones((128, 128), bool)), 0.0, NEG_BIG).astype(np.float32)
    nc = build_nc(4)
    in_maps = []
    for c in range(8):
        m = dict(weights)
        m["x_in"] = xfull[2 * c:2 * c + 2]
        m["ident"] = ident_np
        m["causal"] = causal_np
        in_maps.append(m)
    res = run_bass_kernel_spmd(nc, in_maps, core_ids=list(range(8)))
    out = np.concatenate([r["y_out"] for r in res.results], axis=0)
    return np.ascontiguousarray(out.astype(np.float32))
